# version 09
# baseline (speedup 1.0000x reference)
import contextlib
import numpy as np
import concourse.bass as bass
import concourse.mybir as mybir
from concourse.bass_utils import run_bass_kernel_spmd

F32 = mybir.dt.float32
BF16 = mybir.dt.bfloat16
I32 = mybir.dt.int32
AF = mybir.ActivationFunctionType
ALU = mybir.AluOpType
AX = mybir.AxisListType

D = 2048
KD = 16
T = 1024
TH = 512
NT = 16
E = 8
KI = 16
KH = 16
NPRE = 1
EPS = 1e-5
NCORES = 8
_TEST_STOP = None


class Ph:
    def __init__(self, nc):
        self.nc = nc
        self.cnt = {}
        self.sems = {}
        self._es = contextlib.ExitStack()

    def sem(self, name):
        if name not in self.sems:
            self.sems[name] = self._es.enter_context(self.nc.semaphore(name))
            self.cnt[name] = 0
        return self.sems[name]

    def inc(self, inst, name, n=1):
        inst.then_inc(self.sems[name], n)
        self.cnt[name] += n
        return self.cnt[name]

    def wait(self, eng, name, val):
        if val > 0:
            eng.wait_ge(self.sems[name], val)

    def wait_tok(self, eng, tok):
        if tok is not None:
            self.wait(eng, tok[0], tok[1])


class Res:
    def __init__(self, ph, n):
        self.ph = ph
        self.n = n
        self.use = 0
        self.rtoks = {}

    def acquire(self, eng, rel_tok):
        i = self.use
        self.use += 1
        if i >= self.n:
            tok = self.rtoks[i - self.n]
            assert tok is not None, f"missing rel token for use {i - self.n}"
            self.ph.wait_tok(eng, tok)
        self.rtoks[i] = rel_tok
        return i % self.n

    def set_rel(self, use_idx, tok):
        self.rtoks[use_idx] = tok


class DRes(Res):
    def __init__(self, ph, name, n):
        super().__init__(ph, n)
        self.dn = [f"{name}d{j}" for j in range(n)]
        for s in self.dn:
            ph.sem(s)
        self.loads = [0] * n

    def load(self, eng, dma_fn, rel_tok):
        i = self.use
        j = self.acquire(eng, rel_tok)
        inst = dma_fn(j)
        inst.then_inc(self.ph.sems[self.dn[j]], 16)
        self.ph.cnt[self.dn[j]] += 16
        self.loads[j] += 1
        return j, (self.dn[j], 16 * self.loads[j]), i

    def wait_loaded(self, eng, tok):
        self.ph.wait_tok(eng, tok)


class NS:

    def __init__(self, ph, tag, guard_col=None):
        self.ph = ph
        self.TM = f"{tag}tM"
        self.VM = f"{tag}vM"
        self.SM = f"{tag}sM"
        for s in (self.TM, self.VM, self.SM):
            ph.sem(s)
        self.guard_col = guard_col


def build_program(stop_after=None):
    nc = bass.Bass("TRN2", target_bir_lowering=False)

    x_t = nc.dram_tensor("x_t", [D, 2 * T], F32, kind="ExternalInput").ap()
    meta = nc.dram_tensor("meta", [1, 8], I32, kind="ExternalInput").ap()
    w_attn = nc.dram_tensor("w_attn", [NT, 5, 128, D], F32, kind="ExternalInput").ap()
    w_out = nc.dram_tensor("w_out", [KD, 128, D], F32, kind="ExternalInput").ap()
    w_gate = nc.dram_tensor("w_gate", [128, KD * E], F32, kind="ExternalInput").ap()
    w1p = nc.dram_tensor("w1p", [E, KI, 2, 128, D], F32, kind="ExternalInput").ap()
    w2p = nc.dram_tensor("w2p", [E, KH, 128, D], F32, kind="ExternalInput").ap()
    a_bias = nc.dram_tensor("a_bias", [128, NT], F32, kind="ExternalInput").ap()
    o_bias = nc.dram_tensor("o_bias", [128, KD], F32, kind="ExternalInput").ap()
    rms_r = nc.dram_tensor("rms_r", [128, KD], F32, kind="ExternalInput").ap()
    g_bias = nc.dram_tensor("g_bias", [8, 1], F32, kind="ExternalInput").ap()
    fbias = nc.dram_tensor("fbias", [E, 128, 48], F32, kind="ExternalInput").ap()
    ident = nc.dram_tensor("ident", [128, 128], F32, kind="ExternalInput").ap()
    onehot = nc.dram_tensor("onehot", [8, E * 128], F32, kind="ExternalInput").ap()
    o = nc.dram_tensor("o", [D, T], F32, kind="ExternalOutput").ap()

    sh_pay0 = nc.dram_tensor("sh_pay0", [128, NT], F32, addr_space="Shared")
    sh_pay1 = nc.dram_tensor("sh_pay1", [128, NT], F32, addr_space="Shared")
    sh_flag0 = nc.dram_tensor("sh_flag0", [1, 1], I32, addr_space="Shared")
    sh_flag1 = nc.dram_tensor("sh_flag1", [1, 1], I32, addr_space="Shared")

    xn_bf = nc.alloc_sbuf_tensor("xn_bf", [128, KD * T], BF16)
    xnn_bf = nc.alloc_sbuf_tensor("xnn_bf", [128, KD * T], BF16)
    ye_bf = nc.alloc_sbuf_tensor("ye_bf", [128, KI * T], BF16)
    acc_bf = nc.alloc_sbuf_tensor("acc_bf", [128, KH * T], BF16)
    arena = [nc.alloc_sbuf_tensor(f"wa{j}", [128, 3 * D], BF16) for j in range(2)]
    pool = [nc.alloc_sbuf_tensor(f"wp{j}", [128, T], F32) for j in range(6)]
    P_, Q_, R_ = pool[0], pool[1], pool[2]
    w3, w4, w5 = pool[3], pool[4], pool[5]
    xb = [nc.alloc_sbuf_tensor(f"xb{j}", [128, T], BF16) for j in range(2)]
    xf = nc.alloc_sbuf_tensor("xf", [128, T], F32)
    xg = nc.alloc_sbuf_tensor("xg", [128, T], F32)
    state = nc.alloc_sbuf_tensor("state", [128, NT], F32)
    ab_sb = nc.alloc_sbuf_tensor("ab_sb", [128, NT], F32)
    ob_sb = nc.alloc_sbuf_tensor("ob_sb", [128, KD], F32)
    rms_sb = nc.alloc_sbuf_tensor("rms_sb", [128, KD], F32)
    gb_sb = nc.alloc_sbuf_tensor("gb_sb", [8, 1], F32)
    fb_sb = [nc.alloc_sbuf_tensor(f"fb{j}", [128, 48], F32) for j in range(2)]
    fb2_sb = [nc.alloc_sbuf_tensor(f"fb2_{j}", [128, 48], F32) for j in range(2)]
    wg_sb = nc.alloc_sbuf_tensor("wg_sb", [128, KD * E], F32)
    id_sb = nc.alloc_sbuf_tensor("id_sb", [128, 128], F32)
    oh_sb = nc.alloc_sbuf_tensor("oh_sb", [8, E * 128], F32)
    ones_c = nc.alloc_sbuf_tensor("ones_c", [128, 1], BF16)
    ones_cf = nc.alloc_sbuf_tensor("ones_cf", [128, 1], F32)
    ones_r = nc.alloc_sbuf_tensor("ones_r", [1, 128], F32)
    w8_sb = nc.alloc_sbuf_tensor("w8_sb", [128, T], F32)
    g0_sb = nc.alloc_sbuf_tensor("g0_sb", [8, 1], F32)
    we_sb = nc.alloc_sbuf_tensor("we_sb", [8, T], F32)
    lg_sb = nc.alloc_sbuf_tensor("lg_sb", [8, T], F32)
    tp = {k: nc.alloc_sbuf_tensor(f"tp_{k}", [128, 8], F32)
          for k in ["v8", "eq", "vm", "mge", "ex", "wet"]}
    tc = {k: nc.alloc_sbuf_tensor(f"tc_{k}", [128, 1], F32)
          for k in ["m1", "nm1", "cnt", "m2", "m2p", "e2", "den", "rden"]}
    tc["c2"] = nc.alloc_sbuf_tensor("tc_c2", [128, 1], I32)
    meta_sb = nc.alloc_sbuf_tensor("meta_sb", [1, 8], I32)
    flag_sb = nc.alloc_sbuf_tensor("flag_sb", [1, 1], I32)
    fread = nc.alloc_sbuf_tensor("fread", [1, 1], I32)
    carry_sb = nc.alloc_sbuf_tensor("carry_sb", [128, NT], F32)
    preU = nc.alloc_sbuf_tensor("preU", [128, NT], F32)
    preA = nc.alloc_sbuf_tensor("preA", [128, NT], F32)
    u0rc = nc.alloc_sbuf_tensor("u0rc", [128, NT], F32)

    ps = [nc.alloc_psum_tensor(f"ps{j}", [128, T], F32) for j in range(4)]

    M_ROLE, M_NONCE, M_PRE, M_A, M_B, M_C, M_D, M_CW = range(8)

    with nc.Block() as block:
        ph = Ph(nc)
        for s in ["tM", "vM", "sM", "qC", "qXF", "qXG", "qO",
                  "eqS", "eqP", "eCar", "eqF"]:
            ph.sem(s)

        pF = Res(ph, 4)
        wF = DRes(ph, "wA", 2)
        xbF = DRes(ph, "xB", 2)
        gxF = DRes(ph, "gX", 2)
        fbF = DRes(ph, "fB", 2)
        ens = NS(ph, "e", guard_col=M_PRE)
        epF = Res(ph, 4)
        ewF = DRes(ph, "ewA", 2)
        exB = DRes(ph, "exB", 2)
        mark = {}

        def guard(eng, col):
            if not hasattr(guard, "_n"):
                guard._n = 0
            guard._n += 1
            gname = f"gflag{guard._n}"

            class G:
                def __enter__(self_g):
                    self_g.reg = eng.register(gname)
                    r = self_g.reg.__enter__()
                    eng.reg_load(r, meta_sb[0:1, col:col + 1])
                    self_g.if_g = eng.If_eq(r, 1)
                    self_g.if_g.__enter__()
                    return r

                def __exit__(self_g, *a):
                    self_g.if_g.__exit__(*a)
                    self_g.reg.__exit__(*a)
                    return False
            return G()

        @block.sync
        def _(y):
            for ap_d, sb in [(a_bias, ab_sb), (o_bias, ob_sb), (rms_r, rms_sb),
                             (g_bias, gb_sb), (ident, id_sb), (onehot, oh_sb),
                             (w_gate, wg_sb), (meta, meta_sb),
                             (meta[0:1, M_NONCE:M_NONCE + 1], flag_sb)]:
                y.dma_start(out=sb[:, :], in_=ap_d).then_inc(ph.sems["qC"], 16)
                ph.cnt["qC"] += 16
            mark["qC"] = ph.cnt["qC"]

        @block.vector
        def _(v):
            ph.wait(v, "qC", mark["qC"])
            v.memset(ones_c[:, :], 1.0)
            v.memset(ones_cf[:, :], 1.0)
            v.memset(ones_r[:, :], 1.0)
            v.memset(carry_sb[:, :], 0.0)
            i = v.memset(state[:, :], 0.0)
            ph.inc(i, "vM")
            mark["init_v"] = ph.cnt["vM"]

        @block.tensor
        def _(t):
            ph.wait(t, "vM", mark["init_v"])

        @block.scalar
        def _(s):
            ph.wait(s, "vM", mark["init_v"])

        @block.gpsimd
        def _(g):
            g.wait_ge(ph.sems["vM"], mark["init_v"])

        def norm_generic(col0, own, S, xb_m, pf_m, gcol):
            TM, VM, SM = S.TM, S.VM, S.SM

            def sec(engf, body):
                def wrapped(eng):
                    if gcol is None:
                        body(eng)
                    else:
                        with guard(eng, gcol):
                            body(eng)
                engf(wrapped)

            tM0 = ph.cnt[TM]
            toks1 = []
            mk = {}
            NI = 3 if own else 2

            def b_g1(g):
                for k in range(KD):
                    _, tok, _ = xb_m.load(
                        g, lambda j, k=k: g.dma_start(
                            out=xb[j][:, :],
                            in_=x_t[k * 128:(k + 1) * 128, col0:col0 + T]),
                        (TM, tM0 + NI * k + NI))
                    toks1.append(tok)
            sec(block.gpsimd, b_g1)

            gtoks = []
            if own:
                def b_y1(y):
                    for k in range(KD):
                        _, tok, _ = gxF.load(
                            y, lambda j, k=k: y.dma_start(
                                out=(xf if j == 0 else xg)[:, :],
                                in_=x_t[k * 128:(k + 1) * 128, col0:col0 + T]),
                            (TM, tM0 + NI * k + 2))
                        gtoks.append(tok)
                sec(block.sync, b_y1)

            sm0 = ph.cnt[SM]

            def b_t1(t):
                s3 = pf_m.acquire(t, None)
                s2 = pf_m.acquire(t, None)
                mk["su"] = (pf_m.use - 2, pf_m.use - 1)
                mk["ss"] = (s3, s2)
                if own:
                    sG = pf_m.acquire(t, None)
                    sg0 = pf_m.acquire(t, None)
                    mk["g_uses"] = (pf_m.use - 2, pf_m.use - 1)
                    mk["g_slots"] = (sG, sg0)
                for k in range(KD):
                    xb_m.wait_loaded(t, toks1[k])
                    for h in range(2):
                        i = t.matmul(ps[s3][0:1, h * TH:(h + 1) * TH],
                                     ones_c[:, 0:1],
                                     xb[k % 2][:, h * TH:(h + 1) * TH],
                                     start=(k == 0), stop=(k == KD - 1))
                    assert ph.inc(i, TM) == tM0 + NI * k + 1
                    if own:
                        gxF.wait_loaded(t, gtoks[k])
                        gx = xf if k % 2 == 0 else xg
                        for h in range(2):
                            t.matmul(ps[sG][0:8, h * TH:(h + 1) * TH],
                                     wg_sb[:, k * 8:(k + 1) * 8],
                                     gx[:, h * TH:(h + 1) * TH],
                                     start=(k == 0), stop=(k == KD - 1))
                        i0 = t.matmul(ps[sg0][0:8, 0:1],
                                      wg_sb[:, k * 8:(k + 1) * 8],
                                      ones_cf[:, 0:1],
                                      start=(k == 0), stop=(k == KD - 1))
                        assert ph.inc(i0, TM) == tM0 + NI * k + 2
                        if k == KD - 1:
                            mk["xg_rel"] = (TM, tM0 + NI * k + 2)
                    ph.wait(t, SM, sm0 + k + 1)
                    for h in range(2):
                        i = t.matmul(ps[s2][0:1, h * TH:(h + 1) * TH],
                                     ones_c[:, 0:1],
                                     xb[k % 2][:, h * TH:(h + 1) * TH],
                                     start=(k == 0), stop=(k == KD - 1))
                    assert ph.inc(i, TM) == tM0 + NI * k + NI
                mk["stats_done"] = ph.cnt[TM]
            sec(block.tensor, b_t1)

            def b_s1(s):
                for k in range(KD):
                    ph.wait(s, TM, tM0 + NI * k + (NI - 1))
                    i = s.activation(xb[k % 2][:, :], xb[k % 2][:, :], AF.Square)
                    ph.inc(i, SM)
                    s.wait_ge(ph.sems[SM], ph.cnt[SM])
            sec(block.scalar, b_s1)

            s3, s2 = mk["ss"]
            u3, u2 = mk["su"]

            def b_v1(v):
                def step(inst):
                    ph.inc(inst, VM)
                    v.wait_ge(ph.sems[VM], ph.cnt[VM])
                ph.wait(v, TM, mk["stats_done"])
                step(v.tensor_copy(w3[0:1, :], ps[s3][0:1, :]))
                pf_m.set_rel(u3, (VM, ph.cnt[VM]))
                step(v.tensor_copy(w4[0:1, :], ps[s2][0:1, :]))
                pf_m.set_rel(u2, (VM, ph.cnt[VM]))
                mk["rows"] = ph.cnt[VM]
            sec(block.vector, b_v1)

            vb0 = ph.cnt[VM]

            def b_t2(t):
                ph.wait(t, VM, mk["rows"])
                sA = pf_m.acquire(t, (VM, vb0 + 1))
                for h in range(2):
                    iA = t.matmul(ps[sA][:, h * TH:(h + 1) * TH], ones_r[0:1, :],
                                  w3[0:1, h * TH:(h + 1) * TH], start=True, stop=True)
                ph.inc(iA, TM)
                sB = pf_m.acquire(t, (VM, vb0 + 2))
                for h in range(2):
                    iB = t.matmul(ps[sB][:, h * TH:(h + 1) * TH], ones_r[0:1, :],
                                  w4[0:1, h * TH:(h + 1) * TH], start=True, stop=True)
                ph.inc(iB, TM)
                mk["bcast"] = ph.cnt[TM]
                mk["bc"] = (sA, sB)
            sec(block.tensor, b_t2)

            def b_v2(v):
                sA_, sB_ = mk["bc"]
                def step(inst):
                    ph.inc(inst, VM)
                    v.wait_ge(ph.sems[VM], ph.cnt[VM])
                ph.wait(v, TM, mk["bcast"])
                step(v.tensor_scalar(w3[:, :], ps[sA_][:, :], 1.0 / D, None, ALU.mult))
                step(v.tensor_scalar(w4[:, :], ps[sB_][:, :], 1.0 / D, None, ALU.mult))
                step(v.tensor_mul(w5[:, :], w3[:, :], w3[:, :]))
                step(v.tensor_sub(w4[:, :], w4[:, :], w5[:, :]))
                step(v.tensor_scalar(w4[:, :], w4[:, :], 1e-30, None, ALU.max))
                step(v.tensor_scalar(w5[:, :], w4[:, :], EPS, None, ALU.add))
                mk["vc1"] = ph.cnt[VM]
            sec(block.vector, b_v2)

            def b_s2(s):
                def step(inst):
                    ph.inc(inst, SM)
                    s.wait_ge(ph.sems[SM], ph.cnt[SM])
                ph.wait(s, VM, mk["vc1"])
                step(s.activation(R_[:, :], w5[:, :], AF.Ln))
                step(s.activation(P_[:, :], R_[:, :], AF.Exp, scale=-0.5))
                step(s.activation(w5[:, :], w4[:, :], AF.Ln))
                mk["sc1"] = ph.cnt[SM]
            sec(block.scalar, b_s2)

            def b_v3(v):
                def step(inst):
                    ph.inc(inst, VM)
                    v.wait_ge(ph.sems[VM], ph.cnt[VM])
                ph.wait(v, SM, mk["sc1"])
                step(v.tensor_mul(Q_[:, :], w3[:, :], P_[:, :]))
                step(v.tensor_sub(w5[:, :], w5[:, :], R_[:, :]))
                mk["vc2"] = ph.cnt[VM]
            sec(block.vector, b_v3)

            def b_s3(s):
                ph.wait(s, VM, mk["vc2"])
                i = s.activation(w5[:, :], w5[:, :], AF.Exp)
                ph.inc(i, SM)
                s.wait_ge(ph.sems[SM], ph.cnt[SM])
                mk["sc2"] = ph.cnt[SM]
            sec(block.scalar, b_s3)

            def b_v4(v):
                ph.wait(v, SM, mk["sc2"])
                i = v.tensor_scalar(w5[:, :], w5[:, :], EPS, None, ALU.add)
                ph.inc(i, VM)
                v.wait_ge(ph.sems[VM], ph.cnt[VM])
                mk["vc3"] = ph.cnt[VM]
            sec(block.vector, b_v4)

            def b_s4(s):
                def step(inst):
                    ph.inc(inst, SM)
                    s.wait_ge(ph.sems[SM], ph.cnt[SM])
                ph.wait(s, VM, mk["vc3"])
                step(s.activation(w5[:, :], w5[:, :], AF.Ln))
                step(s.activation(R_[:, :], w5[:, :], AF.Exp, scale=-0.5))
                mk["sc3"] = ph.cnt[SM]
            sec(block.scalar, b_s4)

            def b_v5(v):
                ph.wait(v, SM, mk["sc3"])
                if not own:
                    i = v.tensor_mul(P_[:, :], P_[:, :], R_[:, :])
                    ph.inc(i, VM)
                    v.wait_ge(ph.sems[VM], ph.cnt[VM])
                    i = v.tensor_mul(Q_[:, :], Q_[:, :], R_[:, :])
                    ph.inc(i, VM)
                    v.wait_ge(ph.sems[VM], ph.cnt[VM])
                mk["rowsd"] = ph.cnt[VM]
            sec(block.vector, b_v5)

            vM0 = ph.cnt[VM]
            sm1 = ph.cnt[SM]
            np2 = 3 if own else 2
            ns2 = 2 if own else 1
            toks2 = []
            abufs = [w3, w5]
            bbufs = [w4, xf]

            def b_g2(g):
                for k in range(KD):
                    _, tok, _ = xb_m.load(
                        g, lambda j, k=k: g.dma_start(
                            out=xb[j][:, :],
                            in_=x_t[k * 128:(k + 1) * 128, col0:col0 + T]),
                        (VM, vM0 + np2 * k + 1))
                    toks2.append(tok)
            sec(block.gpsimd, b_g2)

            def b_v6(v):
                def step(inst):
                    ph.inc(inst, VM)
                    v.wait_ge(ph.sems[VM], ph.cnt[VM])
                ph.wait(v, VM, mk["rowsd"])
                for k in range(KD):
                    ab, bb = abufs[k % 2], bbufs[k % 2]
                    if k >= 2:
                        ph.wait(v, SM, sm1 + ns2 * (k - 1))
                    xb_m.wait_loaded(v, toks2[k])
                    step(v.tensor_mul(ab[:, :], xb[k % 2][:, :], P_[:, :]))
                    assert ph.cnt[VM] == vM0 + np2 * k + 1
                    step(v.tensor_sub(bb[:, :], ab[:, :], Q_[:, :]))
                    if own:
                        step(v.tensor_mul(ab[:, :], bb[:, :], R_[:, :]))
                    assert ph.cnt[VM] == vM0 + np2 * (k + 1)
                mk["xnn_v_done"] = ph.cnt[VM]
            sec(block.vector, b_v6)

            def b_s6(s):
                for k in range(KD):
                    ab, bb = abufs[k % 2], bbufs[k % 2]
                    if own:
                        ph.wait(s, VM, vM0 + np2 * k + 2)
                        i = s.activation(xn_bf[:, k * T:(k + 1) * T], bb[:, :],
                                         AF.Copy)
                        ph.inc(i, SM)
                        s.wait_ge(ph.sems[SM], ph.cnt[SM])
                        ph.wait(s, VM, vM0 + np2 * k + 3)
                        i = s.activation(xnn_bf[:, k * T:(k + 1) * T], ab[:, :],
                                         AF.Identity, scale=rms_sb[:, k:k + 1])
                        ph.inc(i, SM)
                        s.wait_ge(ph.sems[SM], ph.cnt[SM])
                    else:
                        ph.wait(s, VM, vM0 + np2 * k + 2)
                        i = s.activation(xnn_bf[:, k * T:(k + 1) * T], bb[:, :],
                                         AF.Identity, scale=rms_sb[:, k:k + 1])
                        ph.inc(i, SM)
                        s.wait_ge(ph.sems[SM], ph.cnt[SM])
                    assert ph.cnt[SM] == sm1 + ns2 * (k + 1)
            sec(block.scalar, b_s6)
            mk["p2s_done"] = ph.cnt[SM]

            if own:
                sG, sg0 = mk["g_slots"]
                uG, ug0 = mk["g_uses"]

                def b_vg(v):
                    def step(inst):
                        ph.inc(inst, VM)
                        v.wait_ge(ph.sems[VM], ph.cnt[VM])
                    ph.wait(v, SM, mk["p2s_done"])
                    step(v.tensor_copy(w3[0:8, :], ps[sG][0:8, :]))
                    pf_m.set_rel(uG, (VM, ph.cnt[VM]))
                    step(v.tensor_copy(g0_sb[0:8, 0:1], ps[sg0][0:8, 0:1]))
                    pf_m.set_rel(ug0, (VM, ph.cnt[VM]))
                    step(v.tensor_mul(w3[0:8, :], w3[0:8, :], P_[0:8, :]))
                    step(v.tensor_scalar(w4[0:8, :], Q_[0:8, :], g0_sb[0:8, 0:1],
                                         None, ALU.mult))
                    step(v.tensor_sub(w3[0:8, :], w3[0:8, :], w4[0:8, :]))
                    mk["lg_pre"] = ph.cnt[VM]
                sec(block.vector, b_vg)

                def b_sg(s):
                    ph.wait(s, VM, mk["lg_pre"])
                    i = s.activation(lg_sb[0:8, :], w3[0:8, :], AF.Identity,
                                     bias=gb_sb[0:8, 0:1])
                    mark["lg"] = ph.inc(i, SM)
                    s.wait_ge(ph.sems[SM], ph.cnt[SM])
                sec(block.scalar, b_sg)
                mark["xf_rel"] = (SM, mk["p2s_done"])
                mark["xg_rel"] = mk["xg_rel"]
                mark["lg_pre_own"] = mk["lg_pre"]

            mk["xnn_done"] = mk["p2s_done"]
            return (mk["xnn_v_done"], mk["p2s_done"])

        def pre_phase():
            xnn_v, xnn_s = norm_generic(0, False, ens, exB, epF, M_PRE)
            TM, VM, SM = ens.TM, ens.VM, ens.SM

            for tau in range(NT):
                st = {}
                sm0 = ph.cnt[SM]
                vm0 = ph.cnt[VM]
                tM0 = ph.cnt[TM]
                ptoks = [(SM, sm0 + 1), (VM, vm0 + 1), (SM, sm0 + 2)]

                def b_g(g, tau=tau, st=st):
                    _, tok, _ = ewF.load(
                        g, lambda j, tau=tau: g.dma_start(
                            out=arena[j][:, 0:3 * D].rearrange(
                                "p (m q) -> p m q", m=3),
                            in_=w_attn[tau, 0:3, :, :].rearrange("m p q -> p m q")),
                        (TM, tM0 + 3))
                    st["tok1"] = tok
                    st["sl1"] = (ewF.use - 1) % 2

                def wg(g):
                    with guard(g, M_PRE):
                        b_g(g)
                block.gpsimd(wg)

                gdone = {}
                gslot = {}

                def b_t(t, tau=tau):
                    ewF.wait_loaded(t, st["tok1"])
                    ph.wait(t, SM, xnn_s)
                    for m in range(3):
                        s_ = epF.acquire(t, ptoks[m])
                        for h in range(2):
                            for k in range(KD):
                                inst = t.matmul(
                                    ps[s_][:, h * TH:(h + 1) * TH],
                                    arena[st["sl1"]][:, m * D + k * 128:m * D + (k + 1) * 128],
                                    xnn_bf[:, k * T + h * TH:k * T + h * TH + TH],
                                    start=(k == 0), stop=(k == KD - 1))
                        gdone[m] = ph.inc(inst, TM)
                        gslot[m] = s_

                def wt(t):
                    with guard(t, M_PRE):
                        b_t(t)
                block.tensor(wt)

                sc = {}

                def b_s(s, tau=tau):
                    ph.wait(s, VM, mark.get("pre_scan_v", 0))
                    ph.wait(s, TM, gdone[0])
                    i = s.activation(w3[:, :], ps[gslot[0]][:, :], AF.Sigmoid,
                                     bias=ab_sb[:, tau:tau + 1])
                    sc["a"] = ph.inc(i, SM)
                    s.wait_ge(ph.sems[SM], ph.cnt[SM])
                    ph.wait(s, TM, gdone[2])
                    i = s.activation(w4[:, :], ps[gslot[2]][:, :], AF.Sigmoid)
                    sc["sd"] = ph.inc(i, SM)
                    s.wait_ge(ph.sems[SM], ph.cnt[SM])

                def ws(s):
                    with guard(s, M_PRE):
                        b_s(s)
                block.scalar(ws)

                def b_v(v, tau=tau):
                    def step(inst):
                        ph.inc(inst, VM)
                        v.wait_ge(ph.sems[VM], ph.cnt[VM])
                    ph.wait(v, SM, sc["sd"])
                    ph.wait(v, TM, gdone[1])
                    step(v.tensor_mul(w5[:, :], ps[gslot[1]][:, :], w4[:, :]))
                    assert ph.cnt[VM] == vm0 + 1
                    ph.wait(v, SM, sc["a"])
                    step(v.tensor_tensor_scan(w4[:, :], w3[:, :], w5[:, :],
                                              u0rc[:, tau:tau + 1],
                                              ALU.mult, ALU.add))
                    step(v.tensor_copy(preU[:, tau:tau + 1], w4[:, 1023:1024]))
                    mark["pre_scan_v"] = ph.cnt[VM]

                def wv(v):
                    with guard(v, M_PRE):
                        b_v(v)
                block.vector(wv)

            pre_v_fin = ph.cnt[ens.VM]

            @block.sync
            def _(y):
                with guard(y, M_D):
                    y.wait_ge(ph.sems[ens.VM], pre_v_fin)
                    y.dma_start(out=sh_pay0.ap(), in_=preU[:, :]
                                ).then_inc(ph.sems["eqS"], 16)
                    y.wait_ge(ph.sems["eqS"], 16)
                    y.dma_start(out=sh_flag0.ap(), in_=flag_sb[:, :]
                                ).then_inc(ph.sems["eqS"], 16)
                    y.wait_ge(ph.sems["eqS"], 32)

            @block.vector
            def _(v):
                with guard(v, M_C):
                    v.wait_ge(ph.sems[ens.VM], pre_v_fin)
                    i = v.tensor_copy(carry_sb[:, :], preU[:, :])
                    ph.inc(i, "eCar")

        _pn = [0]

        def poll(y, flag_dram, sem_w):
            _pn[0] += 1
            p = _pn[0]
            with y.register(f"nonce{p}") as nonce, \
                 y.register(f"fl{p}") as fl, \
                 y.register(f"it{p}") as it, \
                 y.register(f"wv{p}") as wv, \
                 y.register(f"d1{p}") as d1, \
                 y.register(f"cont{p}") as cont:
                y.reg_load(nonce, meta_sb[0:1, M_NONCE:M_NONCE + 1])
                y.reg_mov(fl, 0)
                y.reg_mov(it, 0)
                y.reg_mov(wv, ph.cnt[sem_w])
                y.reg_mov(cont, 1)
                with y.While(cont):
                    y.reg_alu(wv, wv, 16, ALU.add)
                    y.dma_start(out=fread[:, :], in_=flag_dram.ap()
                                ).then_inc(ph.sems[sem_w], 16)
                    y.wait_ge(ph.sems[sem_w], wv)
                    y.reg_load(fl, fread[0:1, 0:1])
                    y.reg_alu(it, it, 1, ALU.add)
                    y.reg_alu(d1, fl, nonce, ALU.not_equal)
                    y.reg_alu(cont, it, 300000, ALU.is_lt)
                    y.reg_alu(cont, cont, d1, ALU.bitwise_and)

        def pre_seed():
            @block.vector
            def _(v):
                i = v.memset(u0rc[:, :], 0.0)
                ph.inc(i, "vM")
                mark["u0z"] = ph.cnt["vM"]

            @block.sync
            def _(y):
                ph.wait(y, "vM", mark["u0z"])
                with guard(y, M_C):
                    poll(y, sh_flag0, "eqF")
                    y.dma_start(out=u0rc[:, :], in_=sh_pay0.ap()
                                ).then_inc(ph.sems["eqP"], 16)
                    y.wait_ge(ph.sems["eqP"], 16)
                    y.dma_start(out=fread[:, :], in_=sh_flag0.ap()
                                ).then_inc(ph.sems["eqP"], 16)
                    y.wait_ge(ph.sems["eqP"], 32)

            @block.vector
            def _(v):
                with guard(v, M_C):
                    v.wait_ge(ph.sems["eqP"], 16)

        def pre_fences():
            fins = [(ens.TM, ph.cnt[ens.TM]), (ens.VM, ph.cnt[ens.VM]),
                    (ens.SM, ph.cnt[ens.SM])]

            for engf in [block.tensor, block.vector, block.scalar,
                         block.gpsimd, block.sync]:
                def body(eng, fins=fins):
                    with guard(eng, M_PRE):
                        for sname, val in fins:
                            ph.wait(eng, sname, val)
                engf(body)

        def emit_top2(tcx):
            vb_c = ph.cnt["vM"]

            @block.tensor
            def _(t):
                ph.wait(t, "sM", mark["lg"])
                sT = pF.acquire(t, ("vM", vb_c + 1))
                i = t.transpose(ps[sT][:, 0:8],
                                lg_sb[0:8, tcx * 128:(tcx + 1) * 128],
                                id_sb[0:8, 0:8])
                mark["tpd"] = ph.inc(i, "tM")
                mark["tps"] = sT

            @block.vector
            def _(v):
                def step(inst):
                    ph.inc(inst, "vM")
                    v.wait_ge(ph.sems["vM"], ph.cnt["vM"])
                sT = mark["tps"]
                ph.wait(v, "tM", mark["tpd"])
                i = v.tensor_copy(tp["v8"][:, :], ps[sT][:, 0:8])
                step(i)
                assert ph.cnt["vM"] == vb_c + 1
                step(v.reduce_max(tc["m1"][:, :], tp["v8"][:, :], axis=AX.X))
                step(v.tensor_scalar(tc["nm1"][:, :], tc["m1"][:, :], -1.0,
                                     None, ALU.mult))
                step(v.tensor_scalar(tp["eq"][:, :], tp["v8"][:, :],
                                     tc["m1"][:, 0:1], None, ALU.is_equal))
                step(v.reduce_sum(tc["cnt"][:, :], tp["eq"][:, :], axis=AX.X))
                step(v.scalar_tensor_tensor(tp["vm"][:, :], tp["eq"][:, :],
                                            -1e30, tp["v8"][:, :],
                                            ALU.mult, ALU.add))
                step(v.reduce_max(tc["m2"][:, :], tp["vm"][:, :], axis=AX.X))
                step(v.tensor_scalar(tc["c2"][:, :], tc["cnt"][:, :], 1.5,
                                     None, ALU.is_gt))
                step(v.tensor_copy(tc["m2p"][:, :], tc["m2"][:, :]))
                step(v.copy_predicated(tc["m2p"][:, :], tc["c2"][:, :],
                                       tc["m1"][:, :]))
                step(v.tensor_scalar(tp["mge"][:, :], tp["v8"][:, :],
                                     tc["m2p"][:, 0:1], None, ALU.is_ge))
                mark["t2m"] = ph.cnt["vM"]

            @block.scalar
            def _(s):
                def step(inst):
                    ph.inc(inst, "sM")
                    s.wait_ge(ph.sems["sM"], ph.cnt["sM"])
                ph.wait(s, "vM", mark["t2m"])
                step(s.activation(tp["ex"][:, :], tp["v8"][:, :], AF.Exp,
                                  bias=tc["nm1"][:, 0:1]))
                step(s.activation(tc["e2"][:, :], tc["m2p"][:, :], AF.Exp,
                                  bias=tc["nm1"][:, 0:1]))
                mark["t2e"] = ph.cnt["sM"]

            @block.vector
            def _(v):
                def step(inst):
                    ph.inc(inst, "vM")
                    v.wait_ge(ph.sems["vM"], ph.cnt["vM"])
                ph.wait(v, "sM", mark["t2e"])
                step(v.tensor_scalar(tc["den"][:, :], tc["e2"][:, :], 1.0,
                                     None, ALU.add))
                step(v.reciprocal(tc["rden"][:, :], tc["den"][:, :]))
                step(v.tensor_mul(tp["wet"][:, :], tp["ex"][:, :],
                                  tp["mge"][:, :]))
                step(v.tensor_scalar(tp["wet"][:, :], tp["wet"][:, :],
                                     tc["rden"][:, 0:1], None, ALU.mult))
                mark["t2w"] = ph.cnt["vM"]
                assert ph.cnt["vM"] == vb_c + 15

            @block.tensor
            def _(t):
                ph.wait(t, "vM", mark["t2w"])
                sT = pF.acquire(t, ("vM", vb_c + 16))
                i = t.transpose(ps[sT][0:8, 0:128], tp["wet"][:, :], id_sb[:, :])
                mark["tpbd"] = ph.inc(i, "tM")
                mark["tpbs"] = sT

            @block.vector
            def _(v):
                sT = mark["tpbs"]
                ph.wait(v, "tM", mark["tpbd"])
                i = v.tensor_copy(we_sb[0:8, tcx * 128:(tcx + 1) * 128],
                                  ps[sT][0:8, 0:128])
                ph.inc(i, "vM")
                v.wait_ge(ph.sems["vM"], ph.cnt["vM"])
                assert ph.cnt["vM"] == vb_c + 16
            if tcx == 7:
                mark["we_done"] = ph.cnt["vM"]

        def attn_own():
            @block.scalar
            def _(s):
                pass

            xnn_v, xnn_s = mark["xnn_done"]
            for tau in range(NT):
                if 1 <= tau <= 8:
                    emit_top2(tau - 1)
                st = {}
                sm0 = ph.cnt["sM"]
                vm0 = ph.cnt["vM"]
                tM0 = ph.cnt["tM"]
                ptoks = [("sM", sm0 + 1), ("vM", vm0 + 1), ("sM", sm0 + 2),
                         ("vM", vm0 + 4), ("vM", vm0 + 5)]

                @block.gpsimd
                def _(g):
                    _, tok, _ = wF.load(
                        g, lambda j, tau=tau: g.dma_start(
                            out=arena[j][:, 0:3 * D].rearrange(
                                "p (m q) -> p m q", m=3),
                            in_=w_attn[tau, 0:3, :, :].rearrange("m p q -> p m q")),
                        ("tM", tM0 + 3))
                    st["tok1"] = tok
                    st["sl1"] = (wF.use - 1) % 2
                    _, tok2, _ = wF.load(
                        g, lambda j, tau=tau: g.dma_start(
                            out=arena[j][:, 0:2 * D].rearrange(
                                "p (m q) -> p m q", m=2),
                            in_=w_attn[tau, 3:5, :, :].rearrange("m p q -> p m q")),
                        ("tM", tM0 + 5))
                    st["tok2"] = tok2
                    st["sl2"] = (wF.use - 1) % 2

                gdone = {}
                gslot = {}

                @block.tensor
                def _(t):
                    wF.wait_loaded(t, st["tok1"])
                    ph.wait(t, "sM", xnn_s)
                    for m in range(5):
                        if m == 3:
                            wF.wait_loaded(t, st["tok2"])
                        sl = st["sl1"] if m < 3 else st["sl2"]
                        off = (m if m < 3 else m - 3) * D
                        s_ = pF.acquire(t, ptoks[m])
                        for h in range(2):
                            for k in range(KD):
                                inst = t.matmul(
                                    ps[s_][:, h * TH:(h + 1) * TH],
                                    arena[sl][:, off + k * 128:off + (k + 1) * 128],
                                    xnn_bf[:, k * T + h * TH:k * T + h * TH + TH],
                                    start=(k == 0), stop=(k == KD - 1))
                        gdone[m] = ph.inc(inst, "tM")
                        assert gdone[m] == tM0 + m + 1
                        gslot[m] = s_

                sc = {}

                @block.scalar
                def _(s):
                    ph.wait(s, "vM", max(mark["lg_pre_own"],
                                         mark.get("scan_v", 0)))
                    ph.wait(s, "tM", gdone[0])
                    i = s.activation(w3[:, :], ps[gslot[0]][:, :], AF.Sigmoid,
                                     bias=ab_sb[:, tau:tau + 1])
                    sc["a"] = ph.inc(i, "sM")
                    assert sc["a"] == sm0 + 1
                    s.wait_ge(ph.sems["sM"], ph.cnt["sM"])
                    ph.wait(s, "tM", gdone[2])
                    i = s.activation(w4[:, :], ps[gslot[2]][:, :], AF.Sigmoid)
                    sc["sd"] = ph.inc(i, "sM")
                    s.wait_ge(ph.sems["sM"], ph.cnt["sM"])

                @block.vector
                def _(v):
                    def step(inst):
                        ph.inc(inst, "vM")
                        v.wait_ge(ph.sems["vM"], ph.cnt["vM"])
                    ph.wait(v, "sM", sc["sd"])
                    ph.wait(v, "tM", gdone[1])
                    step(v.tensor_mul(w5[:, :], ps[gslot[1]][:, :], w4[:, :]))
                    assert ph.cnt["vM"] == vm0 + 1
                    ph.wait(v, "sM", sc["a"])
                    step(v.tensor_tensor_scan(w4[:, :], w3[:, :], w5[:, :],
                                              0.0, ALU.mult, ALU.add))
                    step(v.tensor_tensor_scan(R_[:, :], w3[:, :], w5[:, :],
                                              1.0, ALU.mult, ALU.add))
                    sc["h_done"] = ph.cnt["vM"]

                @block.scalar
                def _(s):
                    ph.wait(s, "tM", gdone[3])
                    ph.wait(s, "vM", sc["h_done"])
                    i = s.activation(w3[:, :], ps[gslot[3]][:, :], AF.Sigmoid)
                    sc["scg"] = ph.inc(i, "sM")
                    s.wait_ge(ph.sems["sM"], ph.cnt["sM"])

                @block.vector
                def _(v):
                    def step(inst):
                        ph.inc(inst, "vM")
                        v.wait_ge(ph.sems["vM"], ph.cnt["vM"])
                    ph.wait(v, "sM", sc["scg"])
                    step(v.tensor_mul(w5[:, :], ps[gslot[3]][:, :], w3[:, :]))
                    assert ph.cnt["vM"] == vm0 + 4
                    ph.wait(v, "tM", gdone[4])
                    step(v.tensor_mul(w5[:, :], w5[:, :], ps[gslot[4]][:, :]))
                    assert ph.cnt["vM"] == vm0 + 5
                    ph.wait(v, "tM", mark.get("ffn_mm_done", 0))
                    step(v.tensor_mul(ye_bf[:, tau * T:(tau + 1) * T],
                                      w5[:, :], w4[:, :]))
                    step(v.tensor_sub(R_[:, :], R_[:, :], w4[:, :]))
                    step(v.tensor_mul(acc_bf[:, tau * T:(tau + 1) * T],
                                      w5[:, :], R_[:, :]))
                    step(v.tensor_copy(preU[:, tau:tau + 1], w4[:, 1023:1024]))
                    step(v.tensor_copy(preA[:, tau:tau + 1], R_[:, 1023:1024]))
                    mark["scan_v"] = ph.cnt["vM"]

        def p4_phase():
            @block.sync
            def _(y):
                with guard(y, M_B):
                    poll(y, sh_flag0, "eqF")
                    y.dma_start(out=carry_sb[:, :], in_=sh_pay0.ap()
                                ).then_inc(ph.sems["eCar"], 16)
                    y.wait_ge(ph.sems["eCar"], 16)
                with guard(y, M_D):
                    poll(y, sh_flag1, "eqF")
                    y.dma_start(out=carry_sb[:, :], in_=sh_pay1.ap()
                                ).then_inc(ph.sems["eCar"], 16)
                    y.wait_ge(ph.sems["eCar"], 16)

            p3_t_fin = ph.cnt["tM"]

            @block.vector
            def _(v):
                def step(inst):
                    ph.inc(inst, "vM")
                    v.wait_ge(ph.sems["vM"], ph.cnt["vM"])
                with guard(v, M_CW):
                    v.wait_ge(ph.sems["eCar"], 1)
                ph.wait(v, "tM", p3_t_fin)
                for tau in range(NT):
                    step(v.scalar_tensor_tensor(
                        xnn_bf[:, tau * T:(tau + 1) * T],
                        acc_bf[:, tau * T:(tau + 1) * T],
                        carry_sb[:, tau:tau + 1],
                        ye_bf[:, tau * T:(tau + 1) * T],
                        ALU.mult, ALU.add))
                step(v.tensor_mul(w3[:, 0:NT], preA[:, :], carry_sb[:, :]))
                step(v.tensor_add(state[:, :], preU[:, :], w3[:, 0:NT]))
                mark["scan_v"] = ph.cnt["vM"]
                mark["y_done"] = ph.cnt["vM"]

            @block.sync
            def _(y):
                with guard(y, M_A):
                    y.wait_ge(ph.sems["vM"], mark["y_done"])
                    y.dma_start(out=sh_pay0.ap(), in_=state[:, :]
                                ).then_inc(ph.sems["eqS"], 16)
                    y.wait_ge(ph.sems["eqS"], 16)
                    y.dma_start(out=sh_flag0.ap(), in_=flag_sb[:, :]
                                ).then_inc(ph.sems["eqS"], 16)
                    y.wait_ge(ph.sems["eqS"], 32)
                with guard(y, M_C):
                    y.wait_ge(ph.sems["vM"], mark["y_done"])
                    y.dma_start(out=sh_pay1.ap(), in_=state[:, :]
                                ).then_inc(ph.sems["eqS"], 16)
                    y.wait_ge(ph.sems["eqS"], 16)
                    y.dma_start(out=sh_flag1.ap(), in_=flag_sb[:, :]
                                ).then_inc(ph.sems["eqS"], 16)
                    y.wait_ge(ph.sems["eqS"], 32)

        def ffn_phase():
            @block.scalar
            def _(s):
                pass

            for e in range(E):
                st = {}

                @block.sync
                def _(y):
                    _, tok, uidx = fbF.load(
                        y, lambda j, e=e: y.dma_start(out=fb_sb[j][:, :],
                                                      in_=fbias[e, :, :]),
                        None)
                    st["tok"] = tok
                    st["uidx"] = uidx
                    st["fb"] = uidx % 2

                @block.vector
                def _(v):
                    sl = st["fb"]
                    fbF.wait_loaded(v, st["tok"])
                    i = v.tensor_scalar(fb2_sb[sl][:, 0:16], fb_sb[sl][:, 0:16],
                                        1.702, None, ALU.mult)
                    ph.inc(i, "vM")
                    v.wait_ge(ph.sems["vM"], ph.cnt["vM"])
                    i = v.tensor_scalar(fb2_sb[sl][:, 16:32], fb_sb[sl][:, 16:32],
                                        1.0, None, ALU.add)
                    ph.inc(i, "vM")
                    v.wait_ge(ph.sems["vM"], ph.cnt["vM"])
                    mark["fbr"] = ph.cnt["vM"]

                vbw = ph.cnt["vM"]

                @block.tensor
                def _(t):
                    ph.wait(t, "vM", mark["we_done"])
                    sW = pF.acquire(t, ("vM", vbw + 1))
                    for h in range(2):
                        i = t.matmul(ps[sW][:, h * TH:(h + 1) * TH],
                                     oh_sb[0:8, e * 128:(e + 1) * 128],
                                     we_sb[0:8, h * TH:(h + 1) * TH],
                                     start=True, stop=True)
                    mark["w8d"] = ph.inc(i, "tM")
                    mark["w8s"] = sW

                @block.vector
                def _(v):
                    sW = mark["w8s"]
                    ph.wait(v, "tM", mark["w8d"])
                    i = v.tensor_copy(w8_sb[:, :], ps[sW][:, :])
                    v_ = ph.inc(i, "vM")
                    assert v_ == vbw + 1
                    v.wait_ge(ph.sems["vM"], ph.cnt["vM"])
                    mark["w8r"] = ph.cnt["vM"]

                for ti in range(KI):
                    stt = {}
                    tb = ph.cnt["tM"]
                    sb = ph.cnt["sM"]

                    @block.gpsimd
                    def _(g):
                        _, tok, _ = wF.load(
                            g,
                            lambda j, e=e, ti=ti: g.dma_start(
                                out=arena[j][:, 0:2 * D].rearrange(
                                    "p (c q) -> p c q", c=2),
                                in_=w1p[e, ti, :, :, :].rearrange("c p q -> p c q")),
                            ("tM", tb + 2))
                        stt["tok"] = tok
                        stt["sl"] = (wF.use - 1) % 2

                    gdone = {}
                    gslot = {}

                    @block.tensor
                    def _(t):
                        sl = stt["sl"]
                        wF.wait_loaded(t, stt["tok"])
                        ptk = [("sM", sb + 2), ("sM", sb + 3)]
                        for m in range(2):
                            s_ = pF.acquire(t, ptk[m])
                            for h in range(2):
                                for k in range(KD):
                                    inst = t.matmul(
                                        ps[s_][:, h * TH:(h + 1) * TH],
                                        arena[sl][:, m * D + k * 128:m * D + (k + 1) * 128],
                                        xn_bf[:, k * T + h * TH:k * T + h * TH + TH],
                                        start=(k == 0), stop=(k == KD - 1))
                            gdone[m] = ph.inc(inst, "tM")
                            assert gdone[m] == tb + m + 1
                            gslot[m] = s_

                    sc = {}

                    @block.scalar
                    def _(s):
                        sl = st["fb"]
                        ph.wait(s, "vM", max(mark["fbr"], mark.get("fpool", 0)))
                        ph.wait(s, "tM", gdone[0])
                        i = s.activation(w3[:, :], ps[gslot[0]][:, :], AF.Sigmoid,
                                         bias=fb2_sb[sl][:, ti:ti + 1], scale=1.702)
                        sc["st"] = ph.inc(i, "sM")
                        assert sc["st"] == sb + 1
                        s.wait_ge(ph.sems["sM"], ph.cnt["sM"])
                        i = s.activation(w4[:, :], ps[gslot[0]][:, :], AF.Identity,
                                         bias=fb_sb[sl][:, ti:ti + 1])
                        sc["xg"] = ph.inc(i, "sM")
                        s.wait_ge(ph.sems["sM"], ph.cnt["sM"])
                        ph.wait(s, "tM", gdone[1])
                        i = s.activation(w5[:, :], ps[gslot[1]][:, :], AF.Identity,
                                         bias=fb2_sb[sl][:, 16 + ti:16 + ti + 1])
                        sc["xl"] = ph.inc(i, "sM")
                        s.wait_ge(ph.sems["sM"], ph.cnt["sM"])

                    @block.vector
                    def _(v):
                        ph.wait(v, "sM", sc["xg"])
                        i = v.tensor_mul(w3[:, :], w4[:, :], w3[:, :])
                        ph.inc(i, "vM")
                        v.wait_ge(ph.sems["vM"], ph.cnt["vM"])
                        ph.wait(v, "sM", sc["xl"])
                        i = v.tensor_mul(ye_bf[:, ti * T:(ti + 1) * T],
                                         w3[:, :], w5[:, :])
                        ph.inc(i, "vM")
                        v.wait_ge(ph.sems["vM"], ph.cnt["vM"])
                        mark["fpool"] = ph.cnt["vM"]
                        mark[f"ye{ti}"] = ph.cnt["vM"]

                for th in range(KH):
                    stt = {}
                    tb = ph.cnt["tM"]
                    vbh = ph.cnt["vM"]

                    @block.gpsimd
                    def _(g):
                        _, tok, _ = wF.load(
                            g,
                            lambda j, e=e, th=th: g.dma_start(
                                out=arena[j][:, 0:D], in_=w2p[e, th, :, :]),
                            ("tM", tb + 1))
                        stt["tok"] = tok
                        stt["sl"] = (wF.use - 1) % 2

                    @block.tensor
                    def _(t):
                        sl = stt["sl"]
                        wF.wait_loaded(t, stt["tok"])
                        ph.wait(t, "vM", mark[f"ye{KI - 1}"])
                        s_ = pF.acquire(t, ("vM", vbh + 1))
                        for h in range(2):
                            for k in range(KI):
                                inst = t.matmul(
                                    ps[s_][:, h * TH:(h + 1) * TH],
                                    arena[sl][:, k * 128:(k + 1) * 128],
                                    ye_bf[:, k * T + h * TH:k * T + h * TH + TH],
                                    start=(k == 0), stop=(k == KI - 1))
                        v_ = ph.inc(inst, "tM")
                        assert v_ == tb + 1
                        mark["w2d"] = v_
                        mark["w2s"] = s_

                    @block.vector
                    def _(v):
                        s_ = mark["w2s"]
                        sl = st["fb"]
                        ph.wait(v, "tM", mark["w2d"])
                        ph.wait(v, "vM", mark["w8r"])
                        i = v.scalar_tensor_tensor(w4[:, :], ps[s_][:, :],
                                                   fb_sb[sl][:, 32 + th:32 + th + 1],
                                                   w8_sb[:, :], ALU.add, ALU.mult)
                        v_ = ph.inc(i, "vM")
                        assert v_ == vbh + 1
                        v.wait_ge(ph.sems["vM"], ph.cnt["vM"])
                        if th == KH - 1:
                            fbF.set_rel(st["uidx"], ("vM", v_))
                        dst = acc_bf[:, th * T:(th + 1) * T]
                        if e == 0:
                            i = v.tensor_copy(dst, w4[:, :])
                        else:
                            i = v.tensor_add(dst, dst, w4[:, :])
                        ph.inc(i, "vM")
                        v.wait_ge(ph.sems["vM"], ph.cnt["vM"])
            mark["ffn_mm_done"] = ph.cnt["tM"]
            mark["ffn_v_done"] = ph.cnt["vM"]

        def out_phase():
            @block.scalar
            def _(s):
                pass

            for td in range(KD):
                stt = {}
                tb = ph.cnt["tM"]
                sb = ph.cnt["sM"]
                buf = xf if td % 2 == 0 else xg

                @block.gpsimd
                def _(g):
                    if td % 2 == 0:
                        _, tok, _ = wF.load(
                            g,
                            lambda j, td=td: g.dma_start(
                                out=arena[j][:, 0:2 * D].rearrange(
                                    "p (a q) -> p a q", a=2),
                                in_=w_out[td:td + 2, :, :].rearrange(
                                    "a p q -> p a q")),
                            ("tM", tb + 2))
                        mark["otok"] = tok
                        mark["ows"] = (wF.use - 1) % 2

                @block.sync
                def _(y):
                    if td == 0:
                        ph.wait_tok(y, mark.get("xf_rel"))
                    if td == 1:
                        ph.wait_tok(y, mark.get("xg_rel"))
                    if td >= 2:
                        ph.wait(y, "qO", 16 * (td - 1))
                    qn = "qXF" if td % 2 == 0 else "qXG"
                    y.dma_start(out=buf[:, :],
                                in_=x_t[td * 128:(td + 1) * 128, NPRE * T:(NPRE + 1) * T]
                                ).then_inc(ph.sems[qn], 16)
                    ph.cnt[qn] += 16
                    stt["qx"] = (qn, ph.cnt[qn])

                @block.tensor
                def _(t):
                    sl = mark["ows"]
                    wF.wait_loaded(t, mark["otok"])
                    ph.wait(t, "vM", mark["scan_v"])
                    off = (td % 2) * D
                    s_ = pF.acquire(t, ("sM", sb + 1))
                    for h in range(2):
                        for k in range(NT):
                            inst = t.matmul(
                                ps[s_][:, h * TH:(h + 1) * TH],
                                arena[sl][:, off + k * 128:off + (k + 1) * 128],
                                xnn_bf[:, k * T + h * TH:k * T + h * TH + TH],
                                start=(k == 0), stop=(k == NT - 1))
                    mark["od"] = ph.inc(inst, "tM")
                    mark["os"] = s_

                sc = {}

                @block.scalar
                def _(s):
                    s_ = mark["os"]
                    ph.wait(s, "vM", max(mark.get("comb", 0), mark["scan_v"]))
                    ph.wait(s, "tM", mark["od"])
                    i = s.activation(w3[:, :], ps[s_][:, :], AF.Identity,
                                     bias=ob_sb[:, td:td + 1])
                    sc["s1"] = ph.inc(i, "sM")
                    assert sc["s1"] == sb + 1
                    s.wait_ge(ph.sems["sM"], ph.cnt["sM"])
                    ph.wait(s, "vM", mark["ffn_v_done"])
                    i = s.activation(w4[:, :], acc_bf[:, td * T:(td + 1) * T], AF.Silu)
                    sc["g"] = ph.inc(i, "sM")
                    s.wait_ge(ph.sems["sM"], ph.cnt["sM"])

                @block.vector
                def _(v):
                    def step(inst):
                        ph.inc(inst, "vM")
                        v.wait_ge(ph.sems["vM"], ph.cnt["vM"])
                    ph.wait(v, stt["qx"][0], stt["qx"][1])
                    step(v.tensor_mul(w5[:, :], buf[:, :], P_[:, :]))
                    ph.wait(v, "sM", sc["s1"])
                    step(v.tensor_add(w5[:, :], w5[:, :], w3[:, :]))
                    step(v.tensor_sub(w5[:, :], w5[:, :], Q_[:, :]))
                    ph.wait(v, "sM", sc["g"])
                    step(v.tensor_mul(w5[:, :], w5[:, :], w4[:, :]))
                    step(v.tensor_add(buf[:, :], buf[:, :], w5[:, :]))
                    mark["comb"] = ph.cnt["vM"]

                @block.sync
                def _(y):
                    ph.wait(y, "vM", mark["comb"])
                    y.dma_start(out=o[td * 128:(td + 1) * 128, :], in_=buf[:, :]
                                ).then_inc(ph.sems["qO"], 16)
                    ph.cnt["qO"] += 16

            @block.sync
            def _(y):
                y.wait_ge(ph.sems["qO"], ph.cnt["qO"])

        def finale_stub():
            @block.vector
            def _(v):
                i = v.memset(w3[:, :], 0.5)
                ph.inc(i, "vM")
                mark["stub"] = ph.cnt["vM"]

            @block.sync
            def _(y):
                ph.wait(y, "vM", mark["stub"])
                for td in range(KD):
                    y.dma_start(out=o[td * 128:(td + 1) * 128, :], in_=w3[:, :]
                                ).then_inc(ph.sems["qO"], 16)
                    ph.cnt["qO"] += 16
                y.wait_ge(ph.sems["qO"], ph.cnt["qO"])

        def norm_own():
            mark["xnn_done"] = norm_generic(T, True, NS(ph, ""), xbF, pF, None)

        def d_fence():
            @block.vector
            def _(v):
                with guard(v, M_D):
                    v.wait_ge(ph.sems["eqS"], 32)

        steps = [
            ("pre_seed", pre_seed),
            ("pre", pre_phase),
            ("fences", pre_fences),
            ("norm_own", norm_own),
            ("d_fence", d_fence),
            ("attn_own", attn_own),
            ("p4", p4_phase),
            ("ffn", ffn_phase),
            ("out", out_phase),
        ]
        done_all = True
        for name, fn in steps:
            fn()
            if stop_after is not None and name == stop_after:
                done_all = False
                break
        if not done_all:
            finale_stub()

    return nc


def kernel(L0_attn_x, L0_attn_rms_w, L0_attn_A_w, L0_attn_A_b, L0_attn_B_w,
           L0_attn_Ca_w, L0_attn_Cb_w, L0_attn_D_w, L0_attn_out_w, L0_attn_out_b,
           L0_ffn_gate_w, L0_ffn_gate_b, L0_ffn_w1, L0_ffn_b1, L0_ffn_w2,
           L0_ffn_b2):
    import secrets
    B, L, _ = L0_attn_x.shape
    x2 = np.ascontiguousarray(np.asarray(L0_attn_x, np.float32).reshape(B * L, D))

    nonce = (secrets.randbits(31) | 1)
    xslabs = []
    metas = []
    for c in range(NCORES):
        beta, r = c // 4, c % 4
        own0 = beta * L + r * T
        slab = np.zeros((2 * T, D), np.float32)
        if r == 2:
            slab[0:T] = x2[beta * L + T:beta * L + 2 * T]
        elif r == 3:
            slab[0:T] = x2[beta * L:beta * L + T]
        slab[T:] = x2[own0:own0 + T]
        xslabs.append(np.ascontiguousarray(slab.T))
        has_pre = 1 if r >= 2 else 0
        cw = 1 if r in (1, 2, 3) else 0
        metas.append(np.array([[r, nonce, has_pre,
                                1 if r == 0 else 0, 1 if r == 1 else 0,
                                1 if r == 2 else 0, 1 if r == 3 else 0,
                                cw]], np.int32))

    def tileT(w):
        n, d = w.shape
        return np.ascontiguousarray(
            w.reshape(n // 128, 128, d // 128, 128).transpose(0, 3, 2, 1))

    wa = np.stack([tileT(np.asarray(w, np.float32)) for w in
                   [L0_attn_A_w, L0_attn_B_w, L0_attn_D_w, L0_attn_Ca_w,
                    L0_attn_Cb_w]], axis=1)
    wa = np.ascontiguousarray(wa.reshape(NT, 5, 128, D))
    wo = np.ascontiguousarray(
        tileT(np.asarray(L0_attn_out_w, np.float32)).reshape(KD, 128, D))
    wg = np.ascontiguousarray(
        np.asarray(L0_ffn_gate_w, np.float32).reshape(E, KD, 128)
        .transpose(2, 1, 0).reshape(128, KD * E))
    w1 = np.asarray(L0_ffn_w1, np.float32)
    w1g = np.ascontiguousarray(w1[:, 0::2, :])
    w1l = np.ascontiguousarray(w1[:, 1::2, :])

    def tileT_e(w):
        eN, iN, dN = w.shape
        return w.reshape(eN, iN // 128, 128, dN // 128, 128).transpose(0, 1, 4, 3, 2)

    w1pk = np.ascontiguousarray(
        np.stack([tileT_e(w1g), tileT_e(w1l)], axis=2).reshape(E, KI, 2, 128, D))
    w2 = np.asarray(L0_ffn_w2, np.float32)
    w2pk = np.ascontiguousarray(
        w2.reshape(E, KH, 128, KI, 128).transpose(0, 1, 4, 3, 2).reshape(E, KH, 128, D))

    col = lambda v: np.ascontiguousarray(np.asarray(v, np.float32).reshape(-1, 128).T)
    ab = col(L0_attn_A_b)
    ob = col(L0_attn_out_b)
    rr = col(L0_attn_rms_w)
    gb = np.ascontiguousarray(np.asarray(L0_ffn_gate_b, np.float32).reshape(8, 1))
    b1 = np.asarray(L0_ffn_b1, np.float32)
    b2 = np.asarray(L0_ffn_b2, np.float32)
    fb = np.stack([col(b1[e, 0::2]) for e in range(E)])
    fbl = np.stack([col(b1[e, 1::2]) for e in range(E)])
    fb2 = np.stack([col(b2[e]) for e in range(E)])
    fbias = np.ascontiguousarray(np.concatenate([fb, fbl, fb2], axis=2))
    idm = np.eye(128, dtype=np.float32)
    oh = np.zeros((8, E * 128), np.float32)
    for e in range(E):
        oh[e, e * 128:(e + 1) * 128] = 1.0

    nc = build_program(stop_after=_TEST_STOP)
    shared = dict(w_attn=wa, w_out=wo, w_gate=wg, w1p=w1pk, w2p=w2pk,
                  a_bias=ab, o_bias=ob, rms_r=rr, g_bias=gb, fbias=fbias,
                  ident=idm, onehot=oh)
    in_maps = [dict(shared, x_t=xslabs[c], meta=metas[c]) for c in range(NCORES)]
    res = run_bass_kernel_spmd(nc, in_maps, core_ids=list(range(NCORES)))
    out = np.concatenate([res.results[c]["o"] for c in range(NCORES)], axis=1)
    return np.ascontiguousarray(out.T).reshape(B, L, D)


# version 10
# speedup vs baseline: 1.2284x; 1.2284x over previous
import contextlib
import numpy as np
import concourse.bass as bass
import concourse.mybir as mybir
from concourse.bass_utils import run_bass_kernel_spmd

F32 = mybir.dt.float32
BF16 = mybir.dt.bfloat16
I32 = mybir.dt.int32
AF = mybir.ActivationFunctionType
ALU = mybir.AluOpType
AX = mybir.AxisListType

D = 2048
KD = 16
T = 1024
TH = 512
NT = 16
E = 8
KI = 16
KH = 16
NPRE = 1
EPS = 1e-5
NCORES = 8
_TEST_STOP = None


class Ph:
    def __init__(self, nc):
        self.nc = nc
        self.cnt = {}
        self.sems = {}
        self._es = contextlib.ExitStack()

    def sem(self, name):
        if name not in self.sems:
            self.sems[name] = self._es.enter_context(self.nc.semaphore(name))
            self.cnt[name] = 0
        return self.sems[name]

    def inc(self, inst, name, n=1):
        inst.then_inc(self.sems[name], n)
        self.cnt[name] += n
        return self.cnt[name]

    def wait(self, eng, name, val):
        if val > 0:
            eng.wait_ge(self.sems[name], val)

    def wait_tok(self, eng, tok):
        if tok is not None:
            self.wait(eng, tok[0], tok[1])


class Res:
    def __init__(self, ph, n):
        self.ph = ph
        self.n = n
        self.use = 0
        self.rtoks = {}

    def acquire(self, eng, rel_tok):
        i = self.use
        self.use += 1
        if i >= self.n:
            tok = self.rtoks[i - self.n]
            assert tok is not None, f"missing rel token for use {i - self.n}"
            self.ph.wait_tok(eng, tok)
        self.rtoks[i] = rel_tok
        return i % self.n

    def set_rel(self, use_idx, tok):
        self.rtoks[use_idx] = tok


class DRes(Res):
    def __init__(self, ph, name, n):
        super().__init__(ph, n)
        self.dn = [f"{name}d{j}" for j in range(n)]
        for s in self.dn:
            ph.sem(s)
        self.loads = [0] * n

    def load(self, eng, dma_fn, rel_tok):
        i = self.use
        j = self.acquire(eng, rel_tok)
        inst = dma_fn(j)
        inst.then_inc(self.ph.sems[self.dn[j]], 16)
        self.ph.cnt[self.dn[j]] += 16
        self.loads[j] += 1
        return j, (self.dn[j], 16 * self.loads[j]), i

    def wait_loaded(self, eng, tok):
        self.ph.wait_tok(eng, tok)


class NS:

    def __init__(self, ph, tag, guard_col=None):
        self.ph = ph
        self.TM = f"{tag}tM"
        self.VM = f"{tag}vM"
        self.SM = f"{tag}sM"
        for s in (self.TM, self.VM, self.SM):
            ph.sem(s)
        self.guard_col = guard_col


def build_program(stop_after=None):
    nc = bass.Bass("TRN2", target_bir_lowering=False)

    x_t = nc.dram_tensor("x_t", [D, 2 * T], F32, kind="ExternalInput").ap()
    meta = nc.dram_tensor("meta", [1, 8], I32, kind="ExternalInput").ap()
    w_attn = nc.dram_tensor("w_attn", [NT, 5, 128, D], F32, kind="ExternalInput").ap()
    w_out = nc.dram_tensor("w_out", [KD, 128, D], F32, kind="ExternalInput").ap()
    w_gate = nc.dram_tensor("w_gate", [128, KD * E], F32, kind="ExternalInput").ap()
    w1p = nc.dram_tensor("w1p", [E, KI, 2, 128, D], F32, kind="ExternalInput").ap()
    w2p = nc.dram_tensor("w2p", [E, KH, 128, D], F32, kind="ExternalInput").ap()
    a_bias = nc.dram_tensor("a_bias", [128, NT], F32, kind="ExternalInput").ap()
    o_bias = nc.dram_tensor("o_bias", [128, KD], F32, kind="ExternalInput").ap()
    rms_r = nc.dram_tensor("rms_r", [128, KD], F32, kind="ExternalInput").ap()
    g_bias = nc.dram_tensor("g_bias", [8, 1], F32, kind="ExternalInput").ap()
    fbias = nc.dram_tensor("fbias", [E, 128, 48], F32, kind="ExternalInput").ap()
    ident = nc.dram_tensor("ident", [128, 128], F32, kind="ExternalInput").ap()
    onehot = nc.dram_tensor("onehot", [8, E * 128], F32, kind="ExternalInput").ap()
    o = nc.dram_tensor("o", [D, T], F32, kind="ExternalOutput").ap()

    sh_pay0 = nc.dram_tensor("sh_pay0", [128, NT], F32, addr_space="Shared")
    sh_pay1 = nc.dram_tensor("sh_pay1", [128, NT], F32, addr_space="Shared")
    sh_flag0 = nc.dram_tensor("sh_flag0", [1, 1], I32, addr_space="Shared")
    sh_flag1 = nc.dram_tensor("sh_flag1", [1, 1], I32, addr_space="Shared")

    xn_bf = nc.alloc_sbuf_tensor("xn_bf", [128, KD * T], BF16)
    xnn_bf = nc.alloc_sbuf_tensor("xnn_bf", [128, KD * T], BF16)
    ye_bf = nc.alloc_sbuf_tensor("ye_bf", [128, KI * T], BF16)
    acc_bf = nc.alloc_sbuf_tensor("acc_bf", [128, KH * T], BF16)
    arena = [nc.alloc_sbuf_tensor(f"wa{j}", [128, 3 * D], BF16) for j in range(2)]
    pool = [nc.alloc_sbuf_tensor(f"wp{j}", [128, T], F32) for j in range(6)]
    P_, Q_, R_ = pool[0], pool[1], pool[2]
    w3, w4, w5 = pool[3], pool[4], pool[5]
    xb = [nc.alloc_sbuf_tensor(f"xb{j}", [128, T], BF16) for j in range(2)]
    xf = nc.alloc_sbuf_tensor("xf", [128, T], F32)
    xg = nc.alloc_sbuf_tensor("xg", [128, T], F32)
    state = nc.alloc_sbuf_tensor("state", [128, NT], F32)
    ab_sb = nc.alloc_sbuf_tensor("ab_sb", [128, NT], F32)
    ob_sb = nc.alloc_sbuf_tensor("ob_sb", [128, KD], F32)
    rms_sb = nc.alloc_sbuf_tensor("rms_sb", [128, KD], F32)
    gb_sb = nc.alloc_sbuf_tensor("gb_sb", [8, 1], F32)
    fb_sb = [nc.alloc_sbuf_tensor(f"fb{j}", [128, 48], F32) for j in range(2)]
    fb2_sb = [nc.alloc_sbuf_tensor(f"fb2_{j}", [128, 48], F32) for j in range(2)]
    wg_sb = nc.alloc_sbuf_tensor("wg_sb", [128, KD * E], F32)
    id_sb = nc.alloc_sbuf_tensor("id_sb", [128, 128], F32)
    oh_sb = nc.alloc_sbuf_tensor("oh_sb", [8, E * 128], F32)
    ones_c = nc.alloc_sbuf_tensor("ones_c", [128, 1], BF16)
    ones_cf = nc.alloc_sbuf_tensor("ones_cf", [128, 1], F32)
    ones_r = nc.alloc_sbuf_tensor("ones_r", [1, 128], F32)
    w8_sb = nc.alloc_sbuf_tensor("w8_sb", [128, T], F32)
    g0_sb = nc.alloc_sbuf_tensor("g0_sb", [8, 1], F32)
    we_sb = nc.alloc_sbuf_tensor("we_sb", [8, T], F32)
    lg_sb = nc.alloc_sbuf_tensor("lg_sb", [8, T], F32)
    t2wet = {c: nc.alloc_sbuf_tensor(f"wet{c}", [128, 8], F32) for c in range(8)}
    tp = {k: nc.alloc_sbuf_tensor(f"tp_{k}", [128, 8], F32)
          for k in ["v8", "eq", "vm", "mge", "ex", "wet"]}
    tc = {k: nc.alloc_sbuf_tensor(f"tc_{k}", [128, 1], F32)
          for k in ["m1", "nm1", "cnt", "m2", "m2p", "e2", "den", "rden"]}
    tc["c2"] = nc.alloc_sbuf_tensor("tc_c2", [128, 1], I32)
    meta_sb = nc.alloc_sbuf_tensor("meta_sb", [1, 8], I32)
    flag_sb = nc.alloc_sbuf_tensor("flag_sb", [1, 1], I32)
    fread = nc.alloc_sbuf_tensor("fread", [1, 1], I32)
    carry_sb = nc.alloc_sbuf_tensor("carry_sb", [128, NT], F32)
    preU = nc.alloc_sbuf_tensor("preU", [128, NT], F32)
    preA = nc.alloc_sbuf_tensor("preA", [128, NT], F32)
    u0rc = nc.alloc_sbuf_tensor("u0rc", [128, NT], F32)

    ps = [nc.alloc_psum_tensor(f"ps{j}", [128, T], F32) for j in range(4)]

    M_ROLE, M_NONCE, M_PRE, M_A, M_B, M_C, M_D, M_CW = range(8)

    with nc.Block() as block:
        ph = Ph(nc)
        for s in ["tM", "vM", "sM", "qC", "qXF", "qXG", "qO",
                  "eqS", "eqP", "eCar", "eqF"]:
            ph.sem(s)

        pF = Res(ph, 4)
        wF = DRes(ph, "wA", 2)
        xbF = DRes(ph, "xB", 2)
        gxF = DRes(ph, "gX", 2)
        fbF = DRes(ph, "fB", 2)
        ens = NS(ph, "e", guard_col=M_PRE)
        epF = Res(ph, 4)
        ewF = DRes(ph, "ewA", 2)
        exB = DRes(ph, "exB", 2)
        mark = {}

        guard_regs = {}

        def guard(eng, col):

            class G:
                def __enter__(self_g):
                    key = (id(eng), col)
                    if key not in guard_regs:
                        ctx = eng.register(f"gflag{col}_{len(guard_regs)}")
                        r = ctx.__enter__()
                        eng.reg_load(r, meta_sb[0:1, col:col + 1])
                        guard_regs[key] = (ctx, r)
                    r = guard_regs[key][1]
                    self_g.if_g = eng.If_eq(r, 1)
                    self_g.if_g.__enter__()
                    return r

                def __exit__(self_g, *a):
                    self_g.if_g.__exit__(*a)
                    return False
            return G()

        @block.sync
        def _(y):
            for ap_d, sb in [(a_bias, ab_sb), (o_bias, ob_sb), (rms_r, rms_sb),
                             (g_bias, gb_sb), (ident, id_sb), (onehot, oh_sb),
                             (w_gate, wg_sb), (meta, meta_sb),
                             (meta[0:1, M_NONCE:M_NONCE + 1], flag_sb)]:
                y.dma_start(out=sb[:, :], in_=ap_d).then_inc(ph.sems["qC"], 16)
                ph.cnt["qC"] += 16
            mark["qC"] = ph.cnt["qC"]

        @block.vector
        def _(v):
            ph.wait(v, "qC", mark["qC"])
            v.memset(ones_c[:, :], 1.0)
            v.memset(ones_cf[:, :], 1.0)
            v.memset(ones_r[:, :], 1.0)
            v.memset(carry_sb[:, :], 0.0)
            i = v.memset(state[:, :], 0.0)
            ph.inc(i, "vM")
            mark["init_v"] = ph.cnt["vM"]

        @block.tensor
        def _(t):
            ph.wait(t, "vM", mark["init_v"])

        @block.scalar
        def _(s):
            ph.wait(s, "vM", mark["init_v"])

        @block.gpsimd
        def _(g):
            g.wait_ge(ph.sems["vM"], mark["init_v"])

        def norm_generic(col0, own, S, xb_m, pf_m, gcol):
            TM, VM, SM = S.TM, S.VM, S.SM

            def sec(engf, body):
                def wrapped(eng):
                    if gcol is None:
                        body(eng)
                    else:
                        with guard(eng, gcol):
                            body(eng)
                engf(wrapped)

            tM0 = ph.cnt[TM]
            toks1 = []
            mk = {}
            NI = 3 if own else 2

            def b_g1(g):
                for k in range(KD):
                    _, tok, _ = xb_m.load(
                        g, lambda j, k=k: g.dma_start(
                            out=xb[j][:, :],
                            in_=x_t[k * 128:(k + 1) * 128, col0:col0 + T]),
                        (TM, tM0 + NI * k + NI))
                    toks1.append(tok)
            sec(block.gpsimd, b_g1)

            gtoks = []
            if own:
                def b_y1(y):
                    for k in range(KD):
                        _, tok, _ = gxF.load(
                            y, lambda j, k=k: y.dma_start(
                                out=(xf if j == 0 else xg)[:, :],
                                in_=x_t[k * 128:(k + 1) * 128, col0:col0 + T]),
                            (TM, tM0 + NI * k + 2))
                        gtoks.append(tok)
                sec(block.sync, b_y1)

            sm0 = ph.cnt[SM]

            def b_t1(t):
                s3 = pf_m.acquire(t, None)
                s2 = pf_m.acquire(t, None)
                mk["su"] = (pf_m.use - 2, pf_m.use - 1)
                mk["ss"] = (s3, s2)
                if own:
                    sG = pf_m.acquire(t, None)
                    sg0 = pf_m.acquire(t, None)
                    mk["g_uses"] = (pf_m.use - 2, pf_m.use - 1)
                    mk["g_slots"] = (sG, sg0)
                for k in range(KD):
                    xb_m.wait_loaded(t, toks1[k])
                    for h in range(2):
                        i = t.matmul(ps[s3][0:1, h * TH:(h + 1) * TH],
                                     ones_c[:, 0:1],
                                     xb[k % 2][:, h * TH:(h + 1) * TH],
                                     start=(k == 0), stop=(k == KD - 1))
                    assert ph.inc(i, TM) == tM0 + NI * k + 1
                    if own:
                        gxF.wait_loaded(t, gtoks[k])
                        gx = xf if k % 2 == 0 else xg
                        for h in range(2):
                            t.matmul(ps[sG][0:8, h * TH:(h + 1) * TH],
                                     wg_sb[:, k * 8:(k + 1) * 8],
                                     gx[:, h * TH:(h + 1) * TH],
                                     start=(k == 0), stop=(k == KD - 1))
                        i0 = t.matmul(ps[sg0][0:8, 0:1],
                                      wg_sb[:, k * 8:(k + 1) * 8],
                                      ones_cf[:, 0:1],
                                      start=(k == 0), stop=(k == KD - 1))
                        assert ph.inc(i0, TM) == tM0 + NI * k + 2
                        if k == KD - 1:
                            mk["xg_rel"] = (TM, tM0 + NI * k + 2)
                    ph.wait(t, SM, sm0 + k + 1)
                    for h in range(2):
                        i = t.matmul(ps[s2][0:1, h * TH:(h + 1) * TH],
                                     ones_c[:, 0:1],
                                     xb[k % 2][:, h * TH:(h + 1) * TH],
                                     start=(k == 0), stop=(k == KD - 1))
                    assert ph.inc(i, TM) == tM0 + NI * k + NI
                mk["stats_done"] = ph.cnt[TM]
            sec(block.tensor, b_t1)

            def b_s1(s):
                for k in range(KD):
                    ph.wait(s, TM, tM0 + NI * k + (NI - 1))
                    i = s.activation(xb[k % 2][:, :], xb[k % 2][:, :], AF.Square)
                    ph.inc(i, SM)
                    s.wait_ge(ph.sems[SM], ph.cnt[SM])
            sec(block.scalar, b_s1)

            s3, s2 = mk["ss"]
            u3, u2 = mk["su"]

            def b_v1(v):
                def step(inst):
                    ph.inc(inst, VM)
                    v.wait_ge(ph.sems[VM], ph.cnt[VM])
                ph.wait(v, TM, mk["stats_done"])
                step(v.tensor_copy(w3[0:1, :], ps[s3][0:1, :]))
                pf_m.set_rel(u3, (VM, ph.cnt[VM]))
                step(v.tensor_copy(w4[0:1, :], ps[s2][0:1, :]))
                pf_m.set_rel(u2, (VM, ph.cnt[VM]))
                mk["rows"] = ph.cnt[VM]
            sec(block.vector, b_v1)

            vb0 = ph.cnt[VM]

            def b_t2(t):
                ph.wait(t, VM, mk["rows"])
                sA = pf_m.acquire(t, (VM, vb0 + 1))
                for h in range(2):
                    iA = t.matmul(ps[sA][:, h * TH:(h + 1) * TH], ones_r[0:1, :],
                                  w3[0:1, h * TH:(h + 1) * TH], start=True, stop=True)
                ph.inc(iA, TM)
                sB = pf_m.acquire(t, (VM, vb0 + 2))
                for h in range(2):
                    iB = t.matmul(ps[sB][:, h * TH:(h + 1) * TH], ones_r[0:1, :],
                                  w4[0:1, h * TH:(h + 1) * TH], start=True, stop=True)
                ph.inc(iB, TM)
                mk["bcast"] = ph.cnt[TM]
                mk["bc"] = (sA, sB)
            sec(block.tensor, b_t2)

            def b_v2(v):
                sA_, sB_ = mk["bc"]
                def step(inst):
                    ph.inc(inst, VM)
                    v.wait_ge(ph.sems[VM], ph.cnt[VM])
                ph.wait(v, TM, mk["bcast"])
                step(v.tensor_scalar(w3[:, :], ps[sA_][:, :], 1.0 / D, None, ALU.mult))
                step(v.tensor_scalar(w4[:, :], ps[sB_][:, :], 1.0 / D, None, ALU.mult))
                step(v.tensor_mul(w5[:, :], w3[:, :], w3[:, :]))
                step(v.tensor_sub(w4[:, :], w4[:, :], w5[:, :]))
                step(v.tensor_scalar(w4[:, :], w4[:, :], 1e-30, None, ALU.max))
                step(v.tensor_scalar(w5[:, :], w4[:, :], EPS, None, ALU.add))
                mk["vc1"] = ph.cnt[VM]
            sec(block.vector, b_v2)

            def b_s2(s):
                def step(inst):
                    ph.inc(inst, SM)
                    s.wait_ge(ph.sems[SM], ph.cnt[SM])
                ph.wait(s, VM, mk["vc1"])
                step(s.activation(R_[:, :], w5[:, :], AF.Ln))
                step(s.activation(P_[:, :], R_[:, :], AF.Exp, scale=-0.5))
                step(s.activation(w5[:, :], w4[:, :], AF.Ln))
                mk["sc1"] = ph.cnt[SM]
            sec(block.scalar, b_s2)

            def b_v3(v):
                def step(inst):
                    ph.inc(inst, VM)
                    v.wait_ge(ph.sems[VM], ph.cnt[VM])
                ph.wait(v, SM, mk["sc1"])
                step(v.tensor_mul(Q_[:, :], w3[:, :], P_[:, :]))
                step(v.tensor_sub(w5[:, :], w5[:, :], R_[:, :]))
                mk["vc2"] = ph.cnt[VM]
            sec(block.vector, b_v3)

            def b_s3(s):
                ph.wait(s, VM, mk["vc2"])
                i = s.activation(w5[:, :], w5[:, :], AF.Exp)
                ph.inc(i, SM)
                s.wait_ge(ph.sems[SM], ph.cnt[SM])
                mk["sc2"] = ph.cnt[SM]
            sec(block.scalar, b_s3)

            def b_v4(v):
                ph.wait(v, SM, mk["sc2"])
                i = v.tensor_scalar(w5[:, :], w5[:, :], EPS, None, ALU.add)
                ph.inc(i, VM)
                v.wait_ge(ph.sems[VM], ph.cnt[VM])
                mk["vc3"] = ph.cnt[VM]
            sec(block.vector, b_v4)

            def b_s4(s):
                def step(inst):
                    ph.inc(inst, SM)
                    s.wait_ge(ph.sems[SM], ph.cnt[SM])
                ph.wait(s, VM, mk["vc3"])
                step(s.activation(w5[:, :], w5[:, :], AF.Ln))
                step(s.activation(R_[:, :], w5[:, :], AF.Exp, scale=-0.5))
                mk["sc3"] = ph.cnt[SM]
            sec(block.scalar, b_s4)

            def b_v5(v):
                ph.wait(v, SM, mk["sc3"])
                if not own:
                    i = v.tensor_mul(P_[:, :], P_[:, :], R_[:, :])
                    ph.inc(i, VM)
                    v.wait_ge(ph.sems[VM], ph.cnt[VM])
                    i = v.tensor_mul(Q_[:, :], Q_[:, :], R_[:, :])
                    ph.inc(i, VM)
                    v.wait_ge(ph.sems[VM], ph.cnt[VM])
                mk["rowsd"] = ph.cnt[VM]
            sec(block.vector, b_v5)

            vM0 = ph.cnt[VM]
            sm1 = ph.cnt[SM]
            np2 = 3 if own else 2
            ns2 = 2 if own else 1
            toks2 = []
            abufs = [w3, w5]
            bbufs = [w4, xf]

            def b_g2(g):
                for k in range(KD):
                    _, tok, _ = xb_m.load(
                        g, lambda j, k=k: g.dma_start(
                            out=xb[j][:, :],
                            in_=x_t[k * 128:(k + 1) * 128, col0:col0 + T]),
                        (VM, vM0 + np2 * k + 1))
                    toks2.append(tok)
            sec(block.gpsimd, b_g2)

            def b_v6(v):
                def step(inst):
                    ph.inc(inst, VM)
                    v.wait_ge(ph.sems[VM], ph.cnt[VM])
                ph.wait(v, VM, mk["rowsd"])
                for k in range(KD):
                    ab, bb = abufs[k % 2], bbufs[k % 2]
                    if k >= 2:
                        ph.wait(v, SM, sm1 + ns2 * (k - 1))
                    xb_m.wait_loaded(v, toks2[k])
                    step(v.tensor_mul(ab[:, :], xb[k % 2][:, :], P_[:, :]))
                    assert ph.cnt[VM] == vM0 + np2 * k + 1
                    step(v.tensor_sub(bb[:, :], ab[:, :], Q_[:, :]))
                    if own:
                        step(v.tensor_mul(ab[:, :], bb[:, :], R_[:, :]))
                    assert ph.cnt[VM] == vM0 + np2 * (k + 1)
                mk["xnn_v_done"] = ph.cnt[VM]
            sec(block.vector, b_v6)

            def b_s6(s):
                for k in range(KD):
                    ab, bb = abufs[k % 2], bbufs[k % 2]
                    if own:
                        ph.wait(s, VM, vM0 + np2 * k + 2)
                        i = s.activation(xn_bf[:, k * T:(k + 1) * T], bb[:, :],
                                         AF.Copy)
                        ph.inc(i, SM)
                        s.wait_ge(ph.sems[SM], ph.cnt[SM])
                        ph.wait(s, VM, vM0 + np2 * k + 3)
                        i = s.activation(xnn_bf[:, k * T:(k + 1) * T], ab[:, :],
                                         AF.Identity, scale=rms_sb[:, k:k + 1])
                        ph.inc(i, SM)
                        s.wait_ge(ph.sems[SM], ph.cnt[SM])
                    else:
                        ph.wait(s, VM, vM0 + np2 * k + 2)
                        i = s.activation(xnn_bf[:, k * T:(k + 1) * T], bb[:, :],
                                         AF.Identity, scale=rms_sb[:, k:k + 1])
                        ph.inc(i, SM)
                        s.wait_ge(ph.sems[SM], ph.cnt[SM])
                    assert ph.cnt[SM] == sm1 + ns2 * (k + 1)
            sec(block.scalar, b_s6)
            mk["p2s_done"] = ph.cnt[SM]

            if own:
                sG, sg0 = mk["g_slots"]
                uG, ug0 = mk["g_uses"]

                def b_vg(v):
                    def step(inst):
                        ph.inc(inst, VM)
                        v.wait_ge(ph.sems[VM], ph.cnt[VM])
                    ph.wait(v, SM, mk["p2s_done"])
                    step(v.tensor_copy(w3[0:8, :], ps[sG][0:8, :]))
                    pf_m.set_rel(uG, (VM, ph.cnt[VM]))
                    step(v.tensor_copy(g0_sb[0:8, 0:1], ps[sg0][0:8, 0:1]))
                    pf_m.set_rel(ug0, (VM, ph.cnt[VM]))
                    step(v.tensor_mul(w3[0:8, :], w3[0:8, :], P_[0:8, :]))
                    step(v.tensor_scalar(w4[0:8, :], Q_[0:8, :], g0_sb[0:8, 0:1],
                                         None, ALU.mult))
                    step(v.tensor_sub(w3[0:8, :], w3[0:8, :], w4[0:8, :]))
                    mk["lg_pre"] = ph.cnt[VM]
                sec(block.vector, b_vg)

                def b_sg(s):
                    ph.wait(s, VM, mk["lg_pre"])
                    i = s.activation(lg_sb[0:8, :], w3[0:8, :], AF.Identity,
                                     bias=gb_sb[0:8, 0:1])
                    mark["lg"] = ph.inc(i, SM)
                    s.wait_ge(ph.sems[SM], ph.cnt[SM])
                sec(block.scalar, b_sg)
                mark["xf_rel"] = (SM, mk["p2s_done"])
                mark["xg_rel"] = mk["xg_rel"]
                mark["lg_pre_own"] = mk["lg_pre"]

            mk["xnn_done"] = mk["p2s_done"]
            return (mk["xnn_v_done"], mk["p2s_done"])

        def pre_phase():
            xnn_v, xnn_s = norm_generic(0, False, ens, exB, epF, M_PRE)
            TM, VM, SM = ens.TM, ens.VM, ens.SM

            for tau in range(NT):
                st = {}
                sm0 = ph.cnt[SM]
                vm0 = ph.cnt[VM]
                tM0 = ph.cnt[TM]
                ptoks = [(SM, sm0 + 1), (VM, vm0 + 1), (SM, sm0 + 2)]

                def b_g(g, tau=tau, st=st):
                    _, tok, _ = ewF.load(
                        g, lambda j, tau=tau: g.dma_start(
                            out=arena[j][:, 0:3 * D].rearrange(
                                "p (m q) -> p m q", m=3),
                            in_=w_attn[tau, 0:3, :, :].rearrange("m p q -> p m q")),
                        (TM, tM0 + 3))
                    st["tok1"] = tok
                    st["sl1"] = (ewF.use - 1) % 2

                def wg(g):
                    with guard(g, M_PRE):
                        b_g(g)
                block.gpsimd(wg)

                gdone = {}
                gslot = {}

                def b_t(t, tau=tau):
                    ewF.wait_loaded(t, st["tok1"])
                    ph.wait(t, SM, xnn_s)
                    for m in range(3):
                        s_ = epF.acquire(t, ptoks[m])
                        for h in range(2):
                            for k in range(KD):
                                inst = t.matmul(
                                    ps[s_][:, h * TH:(h + 1) * TH],
                                    arena[st["sl1"]][:, m * D + k * 128:m * D + (k + 1) * 128],
                                    xnn_bf[:, k * T + h * TH:k * T + h * TH + TH],
                                    start=(k == 0), stop=(k == KD - 1))
                        gdone[m] = ph.inc(inst, TM)
                        gslot[m] = s_

                def wt(t):
                    with guard(t, M_PRE):
                        b_t(t)
                block.tensor(wt)

                sc = {}

                def b_s(s, tau=tau):
                    ph.wait(s, VM, mark.get("pre_scan_v", 0))
                    ph.wait(s, TM, gdone[0])
                    i = s.activation(w3[:, :], ps[gslot[0]][:, :], AF.Sigmoid,
                                     bias=ab_sb[:, tau:tau + 1])
                    sc["a"] = ph.inc(i, SM)
                    s.wait_ge(ph.sems[SM], ph.cnt[SM])
                    ph.wait(s, TM, gdone[2])
                    i = s.activation(w4[:, :], ps[gslot[2]][:, :], AF.Sigmoid)
                    sc["sd"] = ph.inc(i, SM)
                    s.wait_ge(ph.sems[SM], ph.cnt[SM])

                def ws(s):
                    with guard(s, M_PRE):
                        b_s(s)
                block.scalar(ws)

                def b_v(v, tau=tau):
                    def step(inst):
                        ph.inc(inst, VM)
                        v.wait_ge(ph.sems[VM], ph.cnt[VM])
                    ph.wait(v, SM, sc["sd"])
                    ph.wait(v, TM, gdone[1])
                    step(v.tensor_mul(w5[:, :], ps[gslot[1]][:, :], w4[:, :]))
                    assert ph.cnt[VM] == vm0 + 1
                    ph.wait(v, SM, sc["a"])
                    step(v.tensor_tensor_scan(w4[:, :], w3[:, :], w5[:, :],
                                              u0rc[:, tau:tau + 1],
                                              ALU.mult, ALU.add))
                    step(v.tensor_copy(preU[:, tau:tau + 1], w4[:, 1023:1024]))
                    mark["pre_scan_v"] = ph.cnt[VM]

                def wv(v):
                    with guard(v, M_PRE):
                        b_v(v)
                block.vector(wv)

            pre_v_fin = ph.cnt[ens.VM]

            @block.sync
            def _(y):
                with guard(y, M_D):
                    y.wait_ge(ph.sems[ens.VM], pre_v_fin)
                    y.dma_start(out=sh_pay0.ap(), in_=preU[:, :]
                                ).then_inc(ph.sems["eqS"], 16)
                    y.wait_ge(ph.sems["eqS"], 16)
                    y.dma_start(out=sh_flag0.ap(), in_=flag_sb[:, :]
                                ).then_inc(ph.sems["eqS"], 16)
                    y.wait_ge(ph.sems["eqS"], 32)

            @block.vector
            def _(v):
                with guard(v, M_C):
                    v.wait_ge(ph.sems[ens.VM], pre_v_fin)
                    i = v.tensor_copy(carry_sb[:, :], preU[:, :])
                    ph.inc(i, "eCar")

        _pn = [0]

        def poll(y, flag_dram, sem_w):
            _pn[0] += 1
            p = _pn[0]
            with y.register(f"nonce{p}") as nonce, \
                 y.register(f"fl{p}") as fl, \
                 y.register(f"it{p}") as it, \
                 y.register(f"wv{p}") as wv, \
                 y.register(f"d1{p}") as d1, \
                 y.register(f"cont{p}") as cont:
                y.reg_load(nonce, meta_sb[0:1, M_NONCE:M_NONCE + 1])
                y.reg_mov(fl, 0)
                y.reg_mov(it, 0)
                y.reg_mov(wv, ph.cnt[sem_w])
                y.reg_mov(cont, 1)
                with y.While(cont):
                    y.reg_alu(wv, wv, 16, ALU.add)
                    y.dma_start(out=fread[:, :], in_=flag_dram.ap()
                                ).then_inc(ph.sems[sem_w], 16)
                    y.wait_ge(ph.sems[sem_w], wv)
                    y.reg_load(fl, fread[0:1, 0:1])
                    y.reg_alu(it, it, 1, ALU.add)
                    y.reg_alu(d1, fl, nonce, ALU.not_equal)
                    y.reg_alu(cont, it, 300000, ALU.is_lt)
                    y.reg_alu(cont, cont, d1, ALU.bitwise_and)

        def pre_seed():
            @block.vector
            def _(v):
                i = v.memset(u0rc[:, :], 0.0)
                ph.inc(i, "vM")
                mark["u0z"] = ph.cnt["vM"]

            @block.sync
            def _(y):
                ph.wait(y, "vM", mark["u0z"])
                with guard(y, M_C):
                    poll(y, sh_flag0, "eqF")
                    y.dma_start(out=u0rc[:, :], in_=sh_pay0.ap()
                                ).then_inc(ph.sems["eqP"], 16)
                    y.wait_ge(ph.sems["eqP"], 16)
                    y.dma_start(out=fread[:, :], in_=sh_flag0.ap()
                                ).then_inc(ph.sems["eqP"], 16)
                    y.wait_ge(ph.sems["eqP"], 32)

            @block.vector
            def _(v):
                with guard(v, M_C):
                    v.wait_ge(ph.sems["eqP"], 16)

        def pre_fences():
            fins = [(ens.TM, ph.cnt[ens.TM]), (ens.VM, ph.cnt[ens.VM]),
                    (ens.SM, ph.cnt[ens.SM])]

            for engf in [block.tensor, block.vector, block.scalar,
                         block.gpsimd, block.sync]:
                def body(eng, fins=fins):
                    with guard(eng, M_PRE):
                        for sname, val in fins:
                            ph.wait(eng, sname, val)
                engf(body)

        t2w_tok = {}

        def emit_top2_A(tcx):
            vb_c = ph.cnt["vM"]

            @block.tensor
            def _(t):
                ph.wait(t, "sM", mark["lg"])
                sT = pF.acquire(t, ("vM", vb_c + 1))
                i = t.transpose(ps[sT][:, 0:8],
                                lg_sb[0:8, tcx * 128:(tcx + 1) * 128],
                                id_sb[0:8, 0:8])
                mark["tpd"] = ph.inc(i, "tM")
                mark["tps"] = sT

            @block.vector
            def _(v):
                def step(inst):
                    ph.inc(inst, "vM")
                    v.wait_ge(ph.sems["vM"], ph.cnt["vM"])
                sT = mark["tps"]
                ph.wait(v, "tM", mark["tpd"])
                i = v.tensor_copy(tp["v8"][:, :], ps[sT][:, 0:8])
                step(i)
                assert ph.cnt["vM"] == vb_c + 1
                step(v.reduce_max(tc["m1"][:, :], tp["v8"][:, :], axis=AX.X))
                step(v.tensor_scalar(tc["nm1"][:, :], tc["m1"][:, :], -1.0,
                                     None, ALU.mult))
                step(v.tensor_scalar(tp["eq"][:, :], tp["v8"][:, :],
                                     tc["m1"][:, 0:1], None, ALU.is_equal))
                step(v.reduce_sum(tc["cnt"][:, :], tp["eq"][:, :], axis=AX.X))
                step(v.scalar_tensor_tensor(tp["vm"][:, :], tp["eq"][:, :],
                                            -1e30, tp["v8"][:, :],
                                            ALU.mult, ALU.add))
                step(v.reduce_max(tc["m2"][:, :], tp["vm"][:, :], axis=AX.X))
                step(v.tensor_scalar(tc["c2"][:, :], tc["cnt"][:, :], 1.5,
                                     None, ALU.is_gt))
                step(v.tensor_copy(tc["m2p"][:, :], tc["m2"][:, :]))
                step(v.copy_predicated(tc["m2p"][:, :], tc["c2"][:, :],
                                       tc["m1"][:, :]))
                step(v.tensor_scalar(tp["mge"][:, :], tp["v8"][:, :],
                                     tc["m2p"][:, 0:1], None, ALU.is_ge))
                mark["t2m"] = ph.cnt["vM"]

            @block.scalar
            def _(s):
                def step(inst):
                    ph.inc(inst, "sM")
                    s.wait_ge(ph.sems["sM"], ph.cnt["sM"])
                ph.wait(s, "vM", mark["t2m"])
                step(s.activation(tp["ex"][:, :], tp["v8"][:, :], AF.Exp,
                                  bias=tc["nm1"][:, 0:1]))
                step(s.activation(tc["e2"][:, :], tc["m2p"][:, :], AF.Exp,
                                  bias=tc["nm1"][:, 0:1]))
                mark["t2e"] = ph.cnt["sM"]

            @block.vector
            def _(v):
                def step(inst):
                    ph.inc(inst, "vM")
                    v.wait_ge(ph.sems["vM"], ph.cnt["vM"])
                ph.wait(v, "sM", mark["t2e"])
                step(v.tensor_scalar(tc["den"][:, :], tc["e2"][:, :], 1.0,
                                     None, ALU.add))
                step(v.reciprocal(tc["rden"][:, :], tc["den"][:, :]))
                step(v.tensor_mul(t2wet[tcx][:, :], tp["ex"][:, :],
                                  tp["mge"][:, :]))
                step(v.tensor_scalar(t2wet[tcx][:, :], t2wet[tcx][:, :],
                                     tc["rden"][:, 0:1], None, ALU.mult))
                t2w_tok[tcx] = ph.cnt["vM"]

        def emit_top2_B(tcx):
            vb_c = ph.cnt["vM"]

            @block.tensor
            def _(t):
                ph.wait(t, "vM", t2w_tok[tcx])
                sT = pF.acquire(t, ("vM", vb_c + 1))
                i = t.transpose(ps[sT][0:8, 0:128], t2wet[tcx][:, :],
                                id_sb[:, :])
                mark["tpbd"] = ph.inc(i, "tM")
                mark["tpbs"] = sT

            @block.vector
            def _(v):
                sT = mark["tpbs"]
                ph.wait(v, "tM", mark["tpbd"])
                i = v.tensor_copy(we_sb[0:8, tcx * 128:(tcx + 1) * 128],
                                  ps[sT][0:8, 0:128])
                ph.inc(i, "vM")
                v.wait_ge(ph.sems["vM"], ph.cnt["vM"])
                assert ph.cnt["vM"] == vb_c + 1
            if tcx == 7:
                mark["we_done"] = ph.cnt["vM"]

        def attn_own():
            @block.scalar
            def _(s):
                pass

            xnn_v, xnn_s = mark["xnn_done"]
            for tau in range(NT):
                if 2 <= tau <= 9:
                    emit_top2_B(tau - 2)
                if 1 <= tau <= 8:
                    emit_top2_A(tau - 1)
                st = {}
                sm0 = ph.cnt["sM"]
                vm0 = ph.cnt["vM"]
                tM0 = ph.cnt["tM"]
                ptoks = [("sM", sm0 + 1), ("vM", vm0 + 1), ("sM", sm0 + 2),
                         ("vM", vm0 + 4), ("vM", vm0 + 5)]

                @block.gpsimd
                def _(g):
                    _, tok, _ = wF.load(
                        g, lambda j, tau=tau: g.dma_start(
                            out=arena[j][:, 0:3 * D].rearrange(
                                "p (m q) -> p m q", m=3),
                            in_=w_attn[tau, 0:3, :, :].rearrange("m p q -> p m q")),
                        ("tM", tM0 + 3))
                    st["tok1"] = tok
                    st["sl1"] = (wF.use - 1) % 2
                    _, tok2, _ = wF.load(
                        g, lambda j, tau=tau: g.dma_start(
                            out=arena[j][:, 0:2 * D].rearrange(
                                "p (m q) -> p m q", m=2),
                            in_=w_attn[tau, 3:5, :, :].rearrange("m p q -> p m q")),
                        ("tM", tM0 + 5))
                    st["tok2"] = tok2
                    st["sl2"] = (wF.use - 1) % 2

                gdone = {}
                gslot = {}

                @block.tensor
                def _(t):
                    wF.wait_loaded(t, st["tok1"])
                    ph.wait(t, "sM", xnn_s)
                    for m in range(5):
                        if m == 3:
                            wF.wait_loaded(t, st["tok2"])
                        sl = st["sl1"] if m < 3 else st["sl2"]
                        off = (m if m < 3 else m - 3) * D
                        s_ = pF.acquire(t, ptoks[m])
                        for h in range(2):
                            for k in range(KD):
                                inst = t.matmul(
                                    ps[s_][:, h * TH:(h + 1) * TH],
                                    arena[sl][:, off + k * 128:off + (k + 1) * 128],
                                    xnn_bf[:, k * T + h * TH:k * T + h * TH + TH],
                                    start=(k == 0), stop=(k == KD - 1))
                        gdone[m] = ph.inc(inst, "tM")
                        assert gdone[m] == tM0 + m + 1
                        gslot[m] = s_

                sc = {}

                @block.scalar
                def _(s):
                    ph.wait(s, "vM", max(mark["lg_pre_own"],
                                         mark.get("scan_v", 0)))
                    ph.wait(s, "tM", gdone[0])
                    i = s.activation(w3[:, :], ps[gslot[0]][:, :], AF.Sigmoid,
                                     bias=ab_sb[:, tau:tau + 1])
                    sc["a"] = ph.inc(i, "sM")
                    assert sc["a"] == sm0 + 1
                    s.wait_ge(ph.sems["sM"], ph.cnt["sM"])
                    ph.wait(s, "tM", gdone[2])
                    i = s.activation(w4[:, :], ps[gslot[2]][:, :], AF.Sigmoid)
                    sc["sd"] = ph.inc(i, "sM")
                    s.wait_ge(ph.sems["sM"], ph.cnt["sM"])

                @block.vector
                def _(v):
                    def step(inst):
                        ph.inc(inst, "vM")
                        v.wait_ge(ph.sems["vM"], ph.cnt["vM"])
                    ph.wait(v, "sM", sc["sd"])
                    ph.wait(v, "tM", gdone[1])
                    step(v.tensor_mul(w5[:, :], ps[gslot[1]][:, :], w4[:, :]))
                    assert ph.cnt["vM"] == vm0 + 1
                    ph.wait(v, "sM", sc["a"])
                    step(v.tensor_tensor_scan(w4[:, :], w3[:, :], w5[:, :],
                                              0.0, ALU.mult, ALU.add))
                    step(v.tensor_tensor_scan(R_[:, :], w3[:, :], w5[:, :],
                                              1.0, ALU.mult, ALU.add))
                    sc["h_done"] = ph.cnt["vM"]

                @block.scalar
                def _(s):
                    ph.wait(s, "tM", gdone[3])
                    ph.wait(s, "vM", sc["h_done"])
                    i = s.activation(w3[:, :], ps[gslot[3]][:, :], AF.Sigmoid)
                    sc["scg"] = ph.inc(i, "sM")
                    s.wait_ge(ph.sems["sM"], ph.cnt["sM"])

                @block.vector
                def _(v):
                    def step(inst):
                        ph.inc(inst, "vM")
                        v.wait_ge(ph.sems["vM"], ph.cnt["vM"])
                    ph.wait(v, "sM", sc["scg"])
                    step(v.tensor_mul(w5[:, :], ps[gslot[3]][:, :], w3[:, :]))
                    assert ph.cnt["vM"] == vm0 + 4
                    ph.wait(v, "tM", gdone[4])
                    step(v.tensor_mul(w5[:, :], w5[:, :], ps[gslot[4]][:, :]))
                    assert ph.cnt["vM"] == vm0 + 5
                    ph.wait(v, "tM", mark.get("ffn_mm_done", 0))
                    step(v.tensor_mul(ye_bf[:, tau * T:(tau + 1) * T],
                                      w5[:, :], w4[:, :]))
                    step(v.tensor_sub(R_[:, :], R_[:, :], w4[:, :]))
                    step(v.tensor_mul(acc_bf[:, tau * T:(tau + 1) * T],
                                      w5[:, :], R_[:, :]))
                    step(v.tensor_copy(preU[:, tau:tau + 1], w4[:, 1023:1024]))
                    step(v.tensor_copy(preA[:, tau:tau + 1], R_[:, 1023:1024]))
                    mark["scan_v"] = ph.cnt["vM"]

        def p4_phase():
            @block.sync
            def _(y):
                with guard(y, M_B):
                    poll(y, sh_flag0, "eqF")
                    y.dma_start(out=carry_sb[:, :], in_=sh_pay0.ap()
                                ).then_inc(ph.sems["eCar"], 16)
                    y.wait_ge(ph.sems["eCar"], 16)
                with guard(y, M_D):
                    poll(y, sh_flag1, "eqF")
                    y.dma_start(out=carry_sb[:, :], in_=sh_pay1.ap()
                                ).then_inc(ph.sems["eCar"], 16)
                    y.wait_ge(ph.sems["eCar"], 16)

            p3_t_fin = ph.cnt["tM"]

            @block.vector
            def _(v):
                def step(inst):
                    ph.inc(inst, "vM")
                    v.wait_ge(ph.sems["vM"], ph.cnt["vM"])
                with guard(v, M_CW):
                    v.wait_ge(ph.sems["eCar"], 1)
                ph.wait(v, "tM", p3_t_fin)
                for tau in range(NT):
                    step(v.scalar_tensor_tensor(
                        xnn_bf[:, tau * T:(tau + 1) * T],
                        acc_bf[:, tau * T:(tau + 1) * T],
                        carry_sb[:, tau:tau + 1],
                        ye_bf[:, tau * T:(tau + 1) * T],
                        ALU.mult, ALU.add))
                step(v.tensor_mul(w3[:, 0:NT], preA[:, :], carry_sb[:, :]))
                step(v.tensor_add(state[:, :], preU[:, :], w3[:, 0:NT]))
                mark["scan_v"] = ph.cnt["vM"]
                mark["y_done"] = ph.cnt["vM"]

            @block.sync
            def _(y):
                with guard(y, M_A):
                    y.wait_ge(ph.sems["vM"], mark["y_done"])
                    y.dma_start(out=sh_pay0.ap(), in_=state[:, :]
                                ).then_inc(ph.sems["eqS"], 16)
                    y.wait_ge(ph.sems["eqS"], 16)
                    y.dma_start(out=sh_flag0.ap(), in_=flag_sb[:, :]
                                ).then_inc(ph.sems["eqS"], 16)
                    y.wait_ge(ph.sems["eqS"], 32)
                with guard(y, M_C):
                    y.wait_ge(ph.sems["vM"], mark["y_done"])
                    y.dma_start(out=sh_pay1.ap(), in_=state[:, :]
                                ).then_inc(ph.sems["eqS"], 16)
                    y.wait_ge(ph.sems["eqS"], 16)
                    y.dma_start(out=sh_flag1.ap(), in_=flag_sb[:, :]
                                ).then_inc(ph.sems["eqS"], 16)
                    y.wait_ge(ph.sems["eqS"], 32)

        def ffn_phase():
            @block.scalar
            def _(s):
                pass

            for e in range(E):
                st = {}

                @block.sync
                def _(y):
                    _, tok, uidx = fbF.load(
                        y, lambda j, e=e: y.dma_start(out=fb_sb[j][:, :],
                                                      in_=fbias[e, :, :]),
                        None)
                    st["tok"] = tok
                    st["uidx"] = uidx
                    st["fb"] = uidx % 2

                @block.vector
                def _(v):
                    sl = st["fb"]
                    fbF.wait_loaded(v, st["tok"])
                    i = v.tensor_scalar(fb2_sb[sl][:, 0:16], fb_sb[sl][:, 0:16],
                                        1.702, None, ALU.mult)
                    ph.inc(i, "vM")
                    v.wait_ge(ph.sems["vM"], ph.cnt["vM"])
                    i = v.tensor_scalar(fb2_sb[sl][:, 16:32], fb_sb[sl][:, 16:32],
                                        1.0, None, ALU.add)
                    ph.inc(i, "vM")
                    v.wait_ge(ph.sems["vM"], ph.cnt["vM"])
                    mark["fbr"] = ph.cnt["vM"]

                vbw = ph.cnt["vM"]

                @block.tensor
                def _(t):
                    ph.wait(t, "vM", mark["we_done"])
                    sW = pF.acquire(t, ("vM", vbw + 1))
                    for h in range(2):
                        i = t.matmul(ps[sW][:, h * TH:(h + 1) * TH],
                                     oh_sb[0:8, e * 128:(e + 1) * 128],
                                     we_sb[0:8, h * TH:(h + 1) * TH],
                                     start=True, stop=True)
                    mark["w8d"] = ph.inc(i, "tM")
                    mark["w8s"] = sW

                @block.vector
                def _(v):
                    sW = mark["w8s"]
                    ph.wait(v, "tM", mark["w8d"])
                    i = v.tensor_copy(w8_sb[:, :], ps[sW][:, :])
                    v_ = ph.inc(i, "vM")
                    assert v_ == vbw + 1
                    v.wait_ge(ph.sems["vM"], ph.cnt["vM"])
                    mark["w8r"] = ph.cnt["vM"]

                for ti in range(KI):
                    stt = {}
                    tb = ph.cnt["tM"]
                    sb = ph.cnt["sM"]

                    @block.gpsimd
                    def _(g):
                        _, tok, _ = wF.load(
                            g,
                            lambda j, e=e, ti=ti: g.dma_start(
                                out=arena[j][:, 0:2 * D].rearrange(
                                    "p (c q) -> p c q", c=2),
                                in_=w1p[e, ti, :, :, :].rearrange("c p q -> p c q")),
                            ("tM", tb + 2))
                        stt["tok"] = tok
                        stt["sl"] = (wF.use - 1) % 2

                    gdone = {}
                    gslot = {}

                    @block.tensor
                    def _(t):
                        sl = stt["sl"]
                        wF.wait_loaded(t, stt["tok"])
                        ptk = [("sM", sb + 2), ("sM", sb + 3)]
                        for m in range(2):
                            s_ = pF.acquire(t, ptk[m])
                            for h in range(2):
                                for k in range(KD):
                                    inst = t.matmul(
                                        ps[s_][:, h * TH:(h + 1) * TH],
                                        arena[sl][:, m * D + k * 128:m * D + (k + 1) * 128],
                                        xn_bf[:, k * T + h * TH:k * T + h * TH + TH],
                                        start=(k == 0), stop=(k == KD - 1))
                            gdone[m] = ph.inc(inst, "tM")
                            assert gdone[m] == tb + m + 1
                            gslot[m] = s_

                    sc = {}

                    @block.scalar
                    def _(s):
                        sl = st["fb"]
                        ph.wait(s, "vM", max(mark["fbr"], mark.get("fpool", 0)))
                        ph.wait(s, "tM", gdone[0])
                        i = s.activation(w3[:, :], ps[gslot[0]][:, :], AF.Sigmoid,
                                         bias=fb2_sb[sl][:, ti:ti + 1], scale=1.702)
                        sc["st"] = ph.inc(i, "sM")
                        assert sc["st"] == sb + 1
                        s.wait_ge(ph.sems["sM"], ph.cnt["sM"])
                        i = s.activation(w4[:, :], ps[gslot[0]][:, :], AF.Identity,
                                         bias=fb_sb[sl][:, ti:ti + 1])
                        sc["xg"] = ph.inc(i, "sM")
                        s.wait_ge(ph.sems["sM"], ph.cnt["sM"])
                        ph.wait(s, "tM", gdone[1])
                        i = s.activation(w5[:, :], ps[gslot[1]][:, :], AF.Identity,
                                         bias=fb2_sb[sl][:, 16 + ti:16 + ti + 1])
                        sc["xl"] = ph.inc(i, "sM")
                        s.wait_ge(ph.sems["sM"], ph.cnt["sM"])

                    @block.vector
                    def _(v):
                        ph.wait(v, "sM", sc["xg"])
                        i = v.tensor_mul(w3[:, :], w4[:, :], w3[:, :])
                        ph.inc(i, "vM")
                        v.wait_ge(ph.sems["vM"], ph.cnt["vM"])
                        ph.wait(v, "sM", sc["xl"])
                        i = v.tensor_mul(ye_bf[:, ti * T:(ti + 1) * T],
                                         w3[:, :], w5[:, :])
                        ph.inc(i, "vM")
                        v.wait_ge(ph.sems["vM"], ph.cnt["vM"])
                        mark["fpool"] = ph.cnt["vM"]
                        mark[f"ye{ti}"] = ph.cnt["vM"]

                for th in range(KH):
                    stt = {}
                    tb = ph.cnt["tM"]
                    vbh = ph.cnt["vM"]

                    @block.gpsimd
                    def _(g):
                        _, tok, _ = wF.load(
                            g,
                            lambda j, e=e, th=th: g.dma_start(
                                out=arena[j][:, 0:D], in_=w2p[e, th, :, :]),
                            ("tM", tb + 1))
                        stt["tok"] = tok
                        stt["sl"] = (wF.use - 1) % 2

                    @block.tensor
                    def _(t):
                        sl = stt["sl"]
                        wF.wait_loaded(t, stt["tok"])
                        ph.wait(t, "vM", mark[f"ye{KI - 1}"])
                        s_ = pF.acquire(t, ("vM", vbh + 1))
                        for h in range(2):
                            for k in range(KI):
                                inst = t.matmul(
                                    ps[s_][:, h * TH:(h + 1) * TH],
                                    arena[sl][:, k * 128:(k + 1) * 128],
                                    ye_bf[:, k * T + h * TH:k * T + h * TH + TH],
                                    start=(k == 0), stop=(k == KI - 1))
                        v_ = ph.inc(inst, "tM")
                        assert v_ == tb + 1
                        mark["w2d"] = v_
                        mark["w2s"] = s_

                    @block.vector
                    def _(v):
                        s_ = mark["w2s"]
                        sl = st["fb"]
                        ph.wait(v, "tM", mark["w2d"])
                        ph.wait(v, "vM", mark["w8r"])
                        i = v.scalar_tensor_tensor(w4[:, :], ps[s_][:, :],
                                                   fb_sb[sl][:, 32 + th:32 + th + 1],
                                                   w8_sb[:, :], ALU.add, ALU.mult)
                        v_ = ph.inc(i, "vM")
                        assert v_ == vbh + 1
                        v.wait_ge(ph.sems["vM"], ph.cnt["vM"])
                        if th == KH - 1:
                            fbF.set_rel(st["uidx"], ("vM", v_))
                        dst = acc_bf[:, th * T:(th + 1) * T]
                        if e == 0:
                            i = v.tensor_copy(dst, w4[:, :])
                        else:
                            i = v.tensor_add(dst, dst, w4[:, :])
                        ph.inc(i, "vM")
                        v.wait_ge(ph.sems["vM"], ph.cnt["vM"])
            mark["ffn_mm_done"] = ph.cnt["tM"]
            mark["ffn_v_done"] = ph.cnt["vM"]

        def out_phase():
            @block.scalar
            def _(s):
                pass

            for td in range(KD):
                stt = {}
                tb = ph.cnt["tM"]
                sb = ph.cnt["sM"]
                buf = xf if td % 2 == 0 else xg

                @block.gpsimd
                def _(g):
                    if td % 2 == 0:
                        _, tok, _ = wF.load(
                            g,
                            lambda j, td=td: g.dma_start(
                                out=arena[j][:, 0:2 * D].rearrange(
                                    "p (a q) -> p a q", a=2),
                                in_=w_out[td:td + 2, :, :].rearrange(
                                    "a p q -> p a q")),
                            ("tM", tb + 2))
                        mark["otok"] = tok
                        mark["ows"] = (wF.use - 1) % 2

                @block.sync
                def _(y):
                    if td == 0:
                        ph.wait_tok(y, mark.get("xf_rel"))
                    if td == 1:
                        ph.wait_tok(y, mark.get("xg_rel"))
                    if td >= 2:
                        ph.wait(y, "qO", 16 * (td - 1))
                    qn = "qXF" if td % 2 == 0 else "qXG"
                    y.dma_start(out=buf[:, :],
                                in_=x_t[td * 128:(td + 1) * 128, NPRE * T:(NPRE + 1) * T]
                                ).then_inc(ph.sems[qn], 16)
                    ph.cnt[qn] += 16
                    stt["qx"] = (qn, ph.cnt[qn])

                @block.tensor
                def _(t):
                    sl = mark["ows"]
                    wF.wait_loaded(t, mark["otok"])
                    ph.wait(t, "vM", mark["scan_v"])
                    off = (td % 2) * D
                    s_ = pF.acquire(t, ("sM", sb + 1))
                    for h in range(2):
                        for k in range(NT):
                            inst = t.matmul(
                                ps[s_][:, h * TH:(h + 1) * TH],
                                arena[sl][:, off + k * 128:off + (k + 1) * 128],
                                xnn_bf[:, k * T + h * TH:k * T + h * TH + TH],
                                start=(k == 0), stop=(k == NT - 1))
                    mark["od"] = ph.inc(inst, "tM")
                    mark["os"] = s_

                sc = {}
                s1b = w3 if td % 2 == 0 else R_
                gb = w4 if td % 2 == 0 else w8_sb

                @block.scalar
                def _(s):
                    s_ = mark["os"]
                    ph.wait(s, "vM", max(mark.get(f"comb{td - 2}", 0),
                                         mark["scan_v"]))
                    ph.wait(s, "tM", mark["od"])
                    i = s.activation(s1b[:, :], ps[s_][:, :], AF.Identity,
                                     bias=ob_sb[:, td:td + 1])
                    sc["s1"] = ph.inc(i, "sM")
                    assert sc["s1"] == sb + 1
                    s.wait_ge(ph.sems["sM"], ph.cnt["sM"])
                    ph.wait(s, "vM", mark["ffn_v_done"])
                    i = s.activation(gb[:, :], acc_bf[:, td * T:(td + 1) * T],
                                     AF.Silu)
                    sc["g"] = ph.inc(i, "sM")
                    s.wait_ge(ph.sems["sM"], ph.cnt["sM"])

                @block.vector
                def _(v):
                    def step(inst):
                        ph.inc(inst, "vM")
                        v.wait_ge(ph.sems["vM"], ph.cnt["vM"])
                    ph.wait(v, stt["qx"][0], stt["qx"][1])
                    ph.wait(v, "sM", sc["s1"])
                    step(v.tensor_mul(w5[:, :], buf[:, :], P_[:, :]))
                    step(v.tensor_add(w5[:, :], w5[:, :], s1b[:, :]))
                    step(v.tensor_sub(w5[:, :], w5[:, :], Q_[:, :]))
                    ph.wait(v, "sM", sc["g"])
                    step(v.tensor_mul(w5[:, :], w5[:, :], gb[:, :]))
                    step(v.tensor_add(buf[:, :], buf[:, :], w5[:, :]))
                    mark["comb"] = ph.cnt["vM"]
                    mark[f"comb{td}"] = ph.cnt["vM"]

                @block.sync
                def _(y):
                    ph.wait(y, "vM", mark["comb"])
                    y.dma_start(out=o[td * 128:(td + 1) * 128, :], in_=buf[:, :]
                                ).then_inc(ph.sems["qO"], 16)
                    ph.cnt["qO"] += 16

            @block.sync
            def _(y):
                y.wait_ge(ph.sems["qO"], ph.cnt["qO"])

        def finale_stub():
            @block.vector
            def _(v):
                i = v.memset(w3[:, :], 0.5)
                ph.inc(i, "vM")
                mark["stub"] = ph.cnt["vM"]

            @block.sync
            def _(y):
                ph.wait(y, "vM", mark["stub"])
                for td in range(KD):
                    y.dma_start(out=o[td * 128:(td + 1) * 128, :], in_=w3[:, :]
                                ).then_inc(ph.sems["qO"], 16)
                    ph.cnt["qO"] += 16
                y.wait_ge(ph.sems["qO"], ph.cnt["qO"])

        def norm_own():
            mark["xnn_done"] = norm_generic(T, True, NS(ph, ""), xbF, pF, None)

        def d_fence():
            @block.vector
            def _(v):
                with guard(v, M_D):
                    v.wait_ge(ph.sems["eqS"], 32)

        steps = [
            ("pre_seed", pre_seed),
            ("pre", pre_phase),
            ("fences", pre_fences),
            ("norm_own", norm_own),
            ("d_fence", d_fence),
            ("attn_own", attn_own),
            ("p4", p4_phase),
            ("ffn", ffn_phase),
            ("out", out_phase),
        ]
        done_all = True
        for name, fn in steps:
            fn()
            if stop_after is not None and name == stop_after:
                done_all = False
                break
        if not done_all:
            finale_stub()

    return nc


def kernel(L0_attn_x, L0_attn_rms_w, L0_attn_A_w, L0_attn_A_b, L0_attn_B_w,
           L0_attn_Ca_w, L0_attn_Cb_w, L0_attn_D_w, L0_attn_out_w, L0_attn_out_b,
           L0_ffn_gate_w, L0_ffn_gate_b, L0_ffn_w1, L0_ffn_b1, L0_ffn_w2,
           L0_ffn_b2):
    import secrets
    B, L, _ = L0_attn_x.shape
    x2 = np.ascontiguousarray(np.asarray(L0_attn_x, np.float32).reshape(B * L, D))

    nonce = (secrets.randbits(31) | 1)
    xslabs = []
    metas = []
    for c in range(NCORES):
        beta, r = c // 4, c % 4
        own0 = beta * L + r * T
        slab = np.zeros((2 * T, D), np.float32)
        if r == 2:
            slab[0:T] = x2[beta * L + T:beta * L + 2 * T]
        elif r == 3:
            slab[0:T] = x2[beta * L:beta * L + T]
        slab[T:] = x2[own0:own0 + T]
        xslabs.append(np.ascontiguousarray(slab.T))
        has_pre = 1 if r >= 2 else 0
        cw = 1 if r in (1, 2, 3) else 0
        metas.append(np.array([[r, nonce, has_pre,
                                1 if r == 0 else 0, 1 if r == 1 else 0,
                                1 if r == 2 else 0, 1 if r == 3 else 0,
                                cw]], np.int32))

    def tileT(w):
        n, d = w.shape
        return np.ascontiguousarray(
            w.reshape(n // 128, 128, d // 128, 128).transpose(0, 3, 2, 1))

    wa = np.stack([tileT(np.asarray(w, np.float32)) for w in
                   [L0_attn_A_w, L0_attn_B_w, L0_attn_D_w, L0_attn_Ca_w,
                    L0_attn_Cb_w]], axis=1)
    wa = np.ascontiguousarray(wa.reshape(NT, 5, 128, D))
    wo = np.ascontiguousarray(
        tileT(np.asarray(L0_attn_out_w, np.float32)).reshape(KD, 128, D))
    wg = np.ascontiguousarray(
        np.asarray(L0_ffn_gate_w, np.float32).reshape(E, KD, 128)
        .transpose(2, 1, 0).reshape(128, KD * E))
    w1 = np.asarray(L0_ffn_w1, np.float32)
    w1g = np.ascontiguousarray(w1[:, 0::2, :])
    w1l = np.ascontiguousarray(w1[:, 1::2, :])

    def tileT_e(w):
        eN, iN, dN = w.shape
        return w.reshape(eN, iN // 128, 128, dN // 128, 128).transpose(0, 1, 4, 3, 2)

    w1pk = np.ascontiguousarray(
        np.stack([tileT_e(w1g), tileT_e(w1l)], axis=2).reshape(E, KI, 2, 128, D))
    w2 = np.asarray(L0_ffn_w2, np.float32)
    w2pk = np.ascontiguousarray(
        w2.reshape(E, KH, 128, KI, 128).transpose(0, 1, 4, 3, 2).reshape(E, KH, 128, D))

    col = lambda v: np.ascontiguousarray(np.asarray(v, np.float32).reshape(-1, 128).T)
    ab = col(L0_attn_A_b)
    ob = col(L0_attn_out_b)
    rr = col(L0_attn_rms_w)
    gb = np.ascontiguousarray(np.asarray(L0_ffn_gate_b, np.float32).reshape(8, 1))
    b1 = np.asarray(L0_ffn_b1, np.float32)
    b2 = np.asarray(L0_ffn_b2, np.float32)
    fb = np.stack([col(b1[e, 0::2]) for e in range(E)])
    fbl = np.stack([col(b1[e, 1::2]) for e in range(E)])
    fb2 = np.stack([col(b2[e]) for e in range(E)])
    fbias = np.ascontiguousarray(np.concatenate([fb, fbl, fb2], axis=2))
    idm = np.eye(128, dtype=np.float32)
    oh = np.zeros((8, E * 128), np.float32)
    for e in range(E):
        oh[e, e * 128:(e + 1) * 128] = 1.0

    nc = build_program(stop_after=_TEST_STOP)
    shared = dict(w_attn=wa, w_out=wo, w_gate=wg, w1p=w1pk, w2p=w2pk,
                  a_bias=ab, o_bias=ob, rms_r=rr, g_bias=gb, fbias=fbias,
                  ident=idm, onehot=oh)
    in_maps = [dict(shared, x_t=xslabs[c], meta=metas[c]) for c in range(NCORES)]
    res = run_bass_kernel_spmd(nc, in_maps, core_ids=list(range(NCORES)))
    out = np.concatenate([res.results[c]["o"] for c in range(NCORES)], axis=1)
    return np.ascontiguousarray(out.T).reshape(B, L, D)


# version 11
# speedup vs baseline: 1.2355x; 1.0058x over previous
import contextlib
import numpy as np
import concourse.bass as bass
import concourse.mybir as mybir
from concourse.bass_utils import run_bass_kernel_spmd

F32 = mybir.dt.float32
BF16 = mybir.dt.bfloat16
I32 = mybir.dt.int32
AF = mybir.ActivationFunctionType
ALU = mybir.AluOpType
AX = mybir.AxisListType

D = 2048
KD = 16
T = 1024
TH = 512
NT = 16
E = 8
KI = 16
KH = 16
NPRE = 1
EPS = 1e-5
NCORES = 8
_TEST_STOP = None


class Ph:
    def __init__(self, nc):
        self.nc = nc
        self.cnt = {}
        self.sems = {}
        self._es = contextlib.ExitStack()

    def sem(self, name):
        if name not in self.sems:
            self.sems[name] = self._es.enter_context(self.nc.semaphore(name))
            self.cnt[name] = 0
        return self.sems[name]

    def inc(self, inst, name, n=1):
        inst.then_inc(self.sems[name], n)
        self.cnt[name] += n
        return self.cnt[name]

    def wait(self, eng, name, val):
        if val > 0:
            eng.wait_ge(self.sems[name], val)

    def wait_tok(self, eng, tok):
        if tok is not None:
            self.wait(eng, tok[0], tok[1])


class Res:
    def __init__(self, ph, n):
        self.ph = ph
        self.n = n
        self.use = 0
        self.rtoks = {}

    def acquire(self, eng, rel_tok):
        i = self.use
        self.use += 1
        if i >= self.n:
            tok = self.rtoks[i - self.n]
            assert tok is not None, f"missing rel token for use {i - self.n}"
            self.ph.wait_tok(eng, tok)
        self.rtoks[i] = rel_tok
        return i % self.n

    def set_rel(self, use_idx, tok):
        self.rtoks[use_idx] = tok


class DRes(Res):
    def __init__(self, ph, name, n):
        super().__init__(ph, n)
        self.dn = [f"{name}d{j}" for j in range(n)]
        for s in self.dn:
            ph.sem(s)
        self.loads = [0] * n

    def load(self, eng, dma_fn, rel_tok):
        i = self.use
        j = self.acquire(eng, rel_tok)
        inst = dma_fn(j)
        inst.then_inc(self.ph.sems[self.dn[j]], 16)
        self.ph.cnt[self.dn[j]] += 16
        self.loads[j] += 1
        return j, (self.dn[j], 16 * self.loads[j]), i

    def wait_loaded(self, eng, tok):
        self.ph.wait_tok(eng, tok)


class NS:

    def __init__(self, ph, tag, guard_col=None):
        self.ph = ph
        self.TM = f"{tag}tM"
        self.VM = f"{tag}vM"
        self.SM = f"{tag}sM"
        for s in (self.TM, self.VM, self.SM):
            ph.sem(s)
        self.guard_col = guard_col


def build_program(stop_after=None):
    nc = bass.Bass("TRN2", target_bir_lowering=False)

    x_t = nc.dram_tensor("x_t", [D, 2 * T], F32, kind="ExternalInput").ap()
    meta = nc.dram_tensor("meta", [1, 8], I32, kind="ExternalInput").ap()
    w_attn = nc.dram_tensor("w_attn", [NT, 5, 128, D], F32, kind="ExternalInput").ap()
    w_out = nc.dram_tensor("w_out", [KD, 128, D], F32, kind="ExternalInput").ap()
    w_gate = nc.dram_tensor("w_gate", [128, KD * E], F32, kind="ExternalInput").ap()
    w1p = nc.dram_tensor("w1p", [E, KI, 2, 128, D], F32, kind="ExternalInput").ap()
    w2p = nc.dram_tensor("w2p", [E, KH, 128, D], F32, kind="ExternalInput").ap()
    a_bias = nc.dram_tensor("a_bias", [128, NT], F32, kind="ExternalInput").ap()
    o_bias = nc.dram_tensor("o_bias", [128, KD], F32, kind="ExternalInput").ap()
    rms_r = nc.dram_tensor("rms_r", [128, KD], F32, kind="ExternalInput").ap()
    g_bias = nc.dram_tensor("g_bias", [8, 1], F32, kind="ExternalInput").ap()
    fbias = nc.dram_tensor("fbias", [E, 128, 48], F32, kind="ExternalInput").ap()
    ident = nc.dram_tensor("ident", [128, 128], F32, kind="ExternalInput").ap()
    onehot = nc.dram_tensor("onehot", [8, E * 128], F32, kind="ExternalInput").ap()
    o = nc.dram_tensor("o", [D, T], F32, kind="ExternalOutput").ap()

    sh_pay0 = nc.dram_tensor("sh_pay0", [128, NT], F32, addr_space="Shared")
    sh_pay1 = nc.dram_tensor("sh_pay1", [128, NT], F32, addr_space="Shared")
    sh_flag0 = nc.dram_tensor("sh_flag0", [1, 1], I32, addr_space="Shared")
    sh_flag1 = nc.dram_tensor("sh_flag1", [1, 1], I32, addr_space="Shared")

    xn_bf = nc.alloc_sbuf_tensor("xn_bf", [128, KD * T], BF16)
    xnn_bf = nc.alloc_sbuf_tensor("xnn_bf", [128, KD * T], BF16)
    ye_bf = nc.alloc_sbuf_tensor("ye_bf", [128, KI * T], BF16)
    acc_bf = nc.alloc_sbuf_tensor("acc_bf", [128, KH * T], BF16)
    arena = [nc.alloc_sbuf_tensor(f"wa{j}", [128, 3 * D], BF16) for j in range(2)]
    pool = [nc.alloc_sbuf_tensor(f"wp{j}", [128, T], F32) for j in range(6)]
    P_, Q_, R_ = pool[0], pool[1], pool[2]
    w3, w4, w5 = pool[3], pool[4], pool[5]
    xb = [nc.alloc_sbuf_tensor(f"xb{j}", [128, T], BF16) for j in range(2)]
    xf = nc.alloc_sbuf_tensor("xf", [128, T], F32)
    xg = nc.alloc_sbuf_tensor("xg", [128, T], F32)
    state = nc.alloc_sbuf_tensor("state", [128, NT], F32)
    ab_sb = nc.alloc_sbuf_tensor("ab_sb", [128, NT], F32)
    ob_sb = nc.alloc_sbuf_tensor("ob_sb", [128, KD], F32)
    rms_sb = nc.alloc_sbuf_tensor("rms_sb", [128, KD], F32)
    gb_sb = nc.alloc_sbuf_tensor("gb_sb", [8, 1], F32)
    fb_sb = [nc.alloc_sbuf_tensor(f"fb{j}", [128, 48], F32) for j in range(2)]
    fb2_sb = [nc.alloc_sbuf_tensor(f"fb2_{j}", [128, 48], F32) for j in range(2)]
    wg_sb = nc.alloc_sbuf_tensor("wg_sb", [128, KD * E], F32)
    id_sb = nc.alloc_sbuf_tensor("id_sb", [128, 128], F32)
    oh_sb = nc.alloc_sbuf_tensor("oh_sb", [8, E * 128], F32)
    ones_c = nc.alloc_sbuf_tensor("ones_c", [128, 1], BF16)
    ones_cf = nc.alloc_sbuf_tensor("ones_cf", [128, 1], F32)
    ones_r = nc.alloc_sbuf_tensor("ones_r", [1, 128], F32)
    w8_sb = nc.alloc_sbuf_tensor("w8_sb", [128, T], F32)
    g0_sb = nc.alloc_sbuf_tensor("g0_sb", [8, 1], F32)
    we_sb = nc.alloc_sbuf_tensor("we_sb", [8, T], F32)
    lg_sb = nc.alloc_sbuf_tensor("lg_sb", [8, T], F32)
    t2wet = {c: nc.alloc_sbuf_tensor(f"wet{c}", [128, 8], F32) for c in range(8)}
    tp = {k: nc.alloc_sbuf_tensor(f"tp_{k}", [128, 8], F32)
          for k in ["v8", "eq", "vm", "mge", "ex", "wet"]}
    tc = {k: nc.alloc_sbuf_tensor(f"tc_{k}", [128, 1], F32)
          for k in ["m1", "nm1", "cnt", "m2", "m2p", "e2", "den", "rden"]}
    tc["c2"] = nc.alloc_sbuf_tensor("tc_c2", [128, 1], I32)
    meta_sb = nc.alloc_sbuf_tensor("meta_sb", [1, 8], I32)
    flag_sb = nc.alloc_sbuf_tensor("flag_sb", [1, 1], I32)
    fread = nc.alloc_sbuf_tensor("fread", [1, 1], I32)
    carry_sb = nc.alloc_sbuf_tensor("carry_sb", [128, NT], F32)
    preU = nc.alloc_sbuf_tensor("preU", [128, NT], F32)
    preA = nc.alloc_sbuf_tensor("preA", [128, NT], F32)
    u0rc = nc.alloc_sbuf_tensor("u0rc", [128, NT], F32)

    ps = [nc.alloc_psum_tensor(f"ps{j}", [128, T], F32) for j in range(4)]

    M_ROLE, M_NONCE, M_PRE, M_A, M_B, M_C, M_D, M_CW = range(8)

    with nc.Block() as block:
        ph = Ph(nc)
        for s in ["tM", "vM", "sM", "gM", "qC", "qXF", "qXG", "qO",
                  "qGX0", "qGX1", "eqS", "eqP", "eCar", "eqF"]:
            ph.sem(s)

        pF = Res(ph, 4)
        wF = DRes(ph, "wA", 2)
        xbF = DRes(ph, "xB", 2)
        fbF = DRes(ph, "fB", 2)
        ens = NS(ph, "e", guard_col=M_PRE)
        epF = Res(ph, 4)
        ewF = DRes(ph, "ewA", 2)
        exB = DRes(ph, "exB", 2)
        mark = {}

        guard_regs = {}

        def guard(eng, col):

            class G:
                def __enter__(self_g):
                    key = (id(eng), col)
                    if key not in guard_regs:
                        ctx = eng.register(f"gflag{col}_{len(guard_regs)}")
                        r = ctx.__enter__()
                        eng.reg_load(r, meta_sb[0:1, col:col + 1])
                        guard_regs[key] = (ctx, r)
                    r = guard_regs[key][1]
                    self_g.if_g = eng.If_eq(r, 1)
                    self_g.if_g.__enter__()
                    return r

                def __exit__(self_g, *a):
                    self_g.if_g.__exit__(*a)
                    return False
            return G()

        @block.sync
        def _(y):
            for ap_d, sb in [(a_bias, ab_sb), (o_bias, ob_sb), (rms_r, rms_sb),
                             (g_bias, gb_sb), (ident, id_sb), (onehot, oh_sb),
                             (w_gate, wg_sb), (meta, meta_sb),
                             (meta[0:1, M_NONCE:M_NONCE + 1], flag_sb)]:
                y.dma_start(out=sb[:, :], in_=ap_d).then_inc(ph.sems["qC"], 16)
                ph.cnt["qC"] += 16
            mark["qC"] = ph.cnt["qC"]

        @block.vector
        def _(v):
            ph.wait(v, "qC", mark["qC"])
            v.memset(ones_c[:, :], 1.0)
            v.memset(ones_cf[:, :], 1.0)
            v.memset(ones_r[:, :], 1.0)
            v.memset(carry_sb[:, :], 0.0)
            i = v.memset(state[:, :], 0.0)
            ph.inc(i, "vM")
            mark["init_v"] = ph.cnt["vM"]

        @block.tensor
        def _(t):
            ph.wait(t, "vM", mark["init_v"])

        @block.scalar
        def _(s):
            ph.wait(s, "vM", mark["init_v"])

        @block.gpsimd
        def _(g):
            g.wait_ge(ph.sems["vM"], mark["init_v"])

        def norm_generic(col0, own, S, xb_m, pf_m, gcol):
            TM, VM, SM = S.TM, S.VM, S.SM

            def sec(engf, body):
                def wrapped(eng):
                    if gcol is None:
                        body(eng)
                    else:
                        with guard(eng, gcol):
                            body(eng)
                engf(wrapped)

            tM0 = ph.cnt[TM]
            toks1 = []
            mk = {}

            def b_g1(g):
                for k in range(KD):
                    _, tok, _ = xb_m.load(
                        g, lambda j, k=k: g.dma_start(
                            out=xb[j][:, :],
                            in_=x_t[k * 128:(k + 1) * 128, col0:col0 + T]),
                        (TM, tM0 + 2 * k + 2))
                    toks1.append(tok)
            sec(block.gpsimd, b_g1)

            sm0 = ph.cnt[SM]

            def b_t1(t):
                s3 = pf_m.acquire(t, None)
                s2 = pf_m.acquire(t, None)
                mk["su"] = (pf_m.use - 2, pf_m.use - 1)
                mk["ss"] = (s3, s2)
                for k in range(KD):
                    xb_m.wait_loaded(t, toks1[k])
                    for h in range(2):
                        i = t.matmul(ps[s3][0:1, h * TH:(h + 1) * TH],
                                     ones_c[:, 0:1],
                                     xb[k % 2][:, h * TH:(h + 1) * TH],
                                     start=(k == 0), stop=(k == KD - 1))
                    assert ph.inc(i, TM) == tM0 + 2 * k + 1
                    ph.wait(t, SM, sm0 + k + 1)
                    for h in range(2):
                        i = t.matmul(ps[s2][0:1, h * TH:(h + 1) * TH],
                                     ones_c[:, 0:1],
                                     xb[k % 2][:, h * TH:(h + 1) * TH],
                                     start=(k == 0), stop=(k == KD - 1))
                    assert ph.inc(i, TM) == tM0 + 2 * k + 2
                mk["stats_done"] = ph.cnt[TM]
            sec(block.tensor, b_t1)

            def b_s1(s):
                for k in range(KD):
                    ph.wait(s, TM, tM0 + 2 * k + 1)
                    i = s.activation(xb[k % 2][:, :], xb[k % 2][:, :], AF.Square)
                    ph.inc(i, SM)
                    s.wait_ge(ph.sems[SM], ph.cnt[SM])
            sec(block.scalar, b_s1)

            s3, s2 = mk["ss"]
            u3, u2 = mk["su"]

            def b_v1(v):
                def step(inst):
                    ph.inc(inst, VM)
                    v.wait_ge(ph.sems[VM], ph.cnt[VM])
                ph.wait(v, TM, mk["stats_done"])
                step(v.tensor_copy(w3[0:1, :], ps[s3][0:1, :]))
                pf_m.set_rel(u3, (VM, ph.cnt[VM]))
                step(v.tensor_copy(w4[0:1, :], ps[s2][0:1, :]))
                pf_m.set_rel(u2, (VM, ph.cnt[VM]))
                mk["rows"] = ph.cnt[VM]
            sec(block.vector, b_v1)

            vb0 = ph.cnt[VM]

            def b_t2(t):
                ph.wait(t, VM, mk["rows"])
                sA = pf_m.acquire(t, (VM, vb0 + 1))
                for h in range(2):
                    iA = t.matmul(ps[sA][:, h * TH:(h + 1) * TH], ones_r[0:1, :],
                                  w3[0:1, h * TH:(h + 1) * TH], start=True, stop=True)
                ph.inc(iA, TM)
                sB = pf_m.acquire(t, (VM, vb0 + 2))
                for h in range(2):
                    iB = t.matmul(ps[sB][:, h * TH:(h + 1) * TH], ones_r[0:1, :],
                                  w4[0:1, h * TH:(h + 1) * TH], start=True, stop=True)
                ph.inc(iB, TM)
                mk["bcast"] = ph.cnt[TM]
                mk["bc"] = (sA, sB)
            sec(block.tensor, b_t2)

            if own:
                g_tok = {}

                def b_tg(t):
                    sG = pf_m.acquire(t, None)
                    sg0 = pf_m.acquire(t, None)
                    mk["g_uses"] = (pf_m.use - 2, pf_m.use - 1)
                    mk["g_slots"] = (sG, sg0)
                    for k in range(KD):
                        t.wait_ge(ph.sems["qGX0" if k % 2 == 0 else "qGX1"],
                                  16 * (k // 2 + 1))
                        gx = xf if k % 2 == 0 else xg
                        for h in range(2):
                            t.matmul(ps[sG][0:8, h * TH:(h + 1) * TH],
                                     wg_sb[:, k * 8:(k + 1) * 8],
                                     gx[:, h * TH:(h + 1) * TH],
                                     start=(k == 0), stop=(k == KD - 1))
                        i0 = t.matmul(ps[sg0][0:8, 0:1],
                                      wg_sb[:, k * 8:(k + 1) * 8],
                                      ones_cf[:, 0:1],
                                      start=(k == 0), stop=(k == KD - 1))
                        g_tok[k] = ph.inc(i0, TM)
                sec(block.tensor, b_tg)
                mk["xf_rel"] = (TM, g_tok[KD - 2])
                mk["xg_rel"] = (TM, g_tok[KD - 1])

                def b_y1(y):
                    for k in range(KD):
                        if k >= 2:
                            ph.wait(y, TM, g_tok[k - 2])
                        qg = "qGX0" if k % 2 == 0 else "qGX1"
                        y.dma_start(out=(xf if k % 2 == 0 else xg)[:, :],
                                    in_=x_t[k * 128:(k + 1) * 128,
                                            col0:col0 + T]
                                    ).then_inc(ph.sems[qg], 16)
                        ph.cnt[qg] += 16
                sec(block.sync, b_y1)

            def b_v2(v):
                sA_, sB_ = mk["bc"]
                def step(inst):
                    ph.inc(inst, VM)
                    v.wait_ge(ph.sems[VM], ph.cnt[VM])
                ph.wait(v, TM, mk["bcast"])
                step(v.tensor_scalar(w3[:, :], ps[sA_][:, :], 1.0 / D, None, ALU.mult))
                step(v.tensor_scalar(w4[:, :], ps[sB_][:, :], 1.0 / D, None, ALU.mult))
                step(v.tensor_mul(w5[:, :], w3[:, :], w3[:, :]))
                step(v.tensor_sub(w4[:, :], w4[:, :], w5[:, :]))
                step(v.tensor_scalar(w4[:, :], w4[:, :], 1e-30, None, ALU.max))
                step(v.tensor_scalar(w5[:, :], w4[:, :], EPS, None, ALU.add))
                mk["vc1"] = ph.cnt[VM]
            sec(block.vector, b_v2)

            def b_s2(s):
                def step(inst):
                    ph.inc(inst, SM)
                    s.wait_ge(ph.sems[SM], ph.cnt[SM])
                ph.wait(s, VM, mk["vc1"])
                step(s.activation(R_[:, :], w5[:, :], AF.Ln))
                step(s.activation(P_[:, :], R_[:, :], AF.Exp, scale=-0.5))
                step(s.activation(w5[:, :], w4[:, :], AF.Ln))
                mk["sc1"] = ph.cnt[SM]
            sec(block.scalar, b_s2)

            def b_v3(v):
                def step(inst):
                    ph.inc(inst, VM)
                    v.wait_ge(ph.sems[VM], ph.cnt[VM])
                ph.wait(v, SM, mk["sc1"])
                step(v.tensor_mul(Q_[:, :], w3[:, :], P_[:, :]))
                step(v.tensor_sub(w5[:, :], w5[:, :], R_[:, :]))
                mk["vc2"] = ph.cnt[VM]
            sec(block.vector, b_v3)

            def b_s3(s):
                ph.wait(s, VM, mk["vc2"])
                i = s.activation(w5[:, :], w5[:, :], AF.Exp)
                ph.inc(i, SM)
                s.wait_ge(ph.sems[SM], ph.cnt[SM])
                mk["sc2"] = ph.cnt[SM]
            sec(block.scalar, b_s3)

            def b_v4(v):
                ph.wait(v, SM, mk["sc2"])
                i = v.tensor_scalar(w5[:, :], w5[:, :], EPS, None, ALU.add)
                ph.inc(i, VM)
                v.wait_ge(ph.sems[VM], ph.cnt[VM])
                mk["vc3"] = ph.cnt[VM]
            sec(block.vector, b_v4)

            def b_s4(s):
                def step(inst):
                    ph.inc(inst, SM)
                    s.wait_ge(ph.sems[SM], ph.cnt[SM])
                ph.wait(s, VM, mk["vc3"])
                step(s.activation(w5[:, :], w5[:, :], AF.Ln))
                step(s.activation(R_[:, :], w5[:, :], AF.Exp, scale=-0.5))
                mk["sc3"] = ph.cnt[SM]
            sec(block.scalar, b_s4)

            def b_v5(v):
                ph.wait(v, SM, mk["sc3"])
                if not own:
                    i = v.tensor_mul(P_[:, :], P_[:, :], R_[:, :])
                    ph.inc(i, VM)
                    v.wait_ge(ph.sems[VM], ph.cnt[VM])
                    i = v.tensor_mul(Q_[:, :], Q_[:, :], R_[:, :])
                    ph.inc(i, VM)
                    v.wait_ge(ph.sems[VM], ph.cnt[VM])
                mk["rowsd"] = ph.cnt[VM]
            sec(block.vector, b_v5)

            vM0 = ph.cnt[VM]
            sm1 = ph.cnt[SM]
            np2 = 3 if own else 2
            ns2 = 2 if own else 1
            toks2 = []
            abufs = [w3, w5]
            bbufs = [w4, w8_sb]

            def b_g2(g):
                for k in range(KD):
                    _, tok, _ = xb_m.load(
                        g, lambda j, k=k: g.dma_start(
                            out=xb[j][:, :],
                            in_=x_t[k * 128:(k + 1) * 128, col0:col0 + T]),
                        (VM, vM0 + np2 * k + 1))
                    toks2.append(tok)
            sec(block.gpsimd, b_g2)

            def b_v6(v):
                def step(inst):
                    ph.inc(inst, VM)
                    v.wait_ge(ph.sems[VM], ph.cnt[VM])
                ph.wait(v, VM, mk["rowsd"])
                for k in range(KD):
                    ab, bb = abufs[k % 2], bbufs[k % 2]
                    if k >= 2:
                        ph.wait(v, SM, sm1 + ns2 * (k - 1))
                    xb_m.wait_loaded(v, toks2[k])
                    step(v.tensor_mul(ab[:, :], xb[k % 2][:, :], P_[:, :]))
                    assert ph.cnt[VM] == vM0 + np2 * k + 1
                    step(v.tensor_sub(bb[:, :], ab[:, :], Q_[:, :]))
                    if own:
                        step(v.tensor_mul(ab[:, :], bb[:, :], R_[:, :]))
                    assert ph.cnt[VM] == vM0 + np2 * (k + 1)
                mk["xnn_v_done"] = ph.cnt[VM]
            sec(block.vector, b_v6)

            def b_s6(s):
                for k in range(KD):
                    ab, bb = abufs[k % 2], bbufs[k % 2]
                    if own:
                        ph.wait(s, VM, vM0 + np2 * k + 2)
                        i = s.activation(xn_bf[:, k * T:(k + 1) * T], bb[:, :],
                                         AF.Copy)
                        ph.inc(i, SM)
                        s.wait_ge(ph.sems[SM], ph.cnt[SM])
                        ph.wait(s, VM, vM0 + np2 * k + 3)
                        i = s.activation(xnn_bf[:, k * T:(k + 1) * T], ab[:, :],
                                         AF.Identity, scale=rms_sb[:, k:k + 1])
                        ph.inc(i, SM)
                        s.wait_ge(ph.sems[SM], ph.cnt[SM])
                    else:
                        ph.wait(s, VM, vM0 + np2 * k + 2)
                        i = s.activation(xnn_bf[:, k * T:(k + 1) * T], bb[:, :],
                                         AF.Identity, scale=rms_sb[:, k:k + 1])
                        ph.inc(i, SM)
                        s.wait_ge(ph.sems[SM], ph.cnt[SM])
                    assert ph.cnt[SM] == sm1 + ns2 * (k + 1)
            sec(block.scalar, b_s6)
            mk["p2s_done"] = ph.cnt[SM]

            if own:
                sG, sg0 = mk["g_slots"]
                uG, ug0 = mk["g_uses"]

                def b_vg(v):
                    def step(inst):
                        ph.inc(inst, VM)
                        v.wait_ge(ph.sems[VM], ph.cnt[VM])
                    ph.wait(v, SM, mk["p2s_done"])
                    step(v.tensor_copy(w3[0:8, :], ps[sG][0:8, :]))
                    pf_m.set_rel(uG, (VM, ph.cnt[VM]))
                    step(v.tensor_copy(g0_sb[0:8, 0:1], ps[sg0][0:8, 0:1]))
                    pf_m.set_rel(ug0, (VM, ph.cnt[VM]))
                    step(v.tensor_mul(w3[0:8, :], w3[0:8, :], P_[0:8, :]))
                    step(v.tensor_scalar(w4[0:8, :], Q_[0:8, :], g0_sb[0:8, 0:1],
                                         None, ALU.mult))
                    step(v.tensor_sub(w3[0:8, :], w3[0:8, :], w4[0:8, :]))
                    mk["lg_pre"] = ph.cnt[VM]
                sec(block.vector, b_vg)

                def b_sg(s):
                    ph.wait(s, VM, mk["lg_pre"])
                    i = s.activation(lg_sb[0:8, :], w3[0:8, :], AF.Identity,
                                     bias=gb_sb[0:8, 0:1])
                    mark["lg"] = ph.inc(i, SM)
                    s.wait_ge(ph.sems[SM], ph.cnt[SM])
                sec(block.scalar, b_sg)
                mark["xf_rel"] = mk["xf_rel"]
                mark["xg_rel"] = mk["xg_rel"]
                mark["lg_pre_own"] = mk["lg_pre"]

            mk["xnn_done"] = mk["p2s_done"]
            return (mk["xnn_v_done"], mk["p2s_done"])

        def pre_phase():
            xnn_v, xnn_s = norm_generic(0, False, ens, exB, epF, M_PRE)
            TM, VM, SM = ens.TM, ens.VM, ens.SM

            for tau in range(NT):
                st = {}
                sm0 = ph.cnt[SM]
                vm0 = ph.cnt[VM]
                tM0 = ph.cnt[TM]
                ptoks = [(SM, sm0 + 1), (VM, vm0 + 1), (SM, sm0 + 2)]

                def b_g(g, tau=tau, st=st):
                    _, tok, _ = ewF.load(
                        g, lambda j, tau=tau: g.dma_start(
                            out=arena[j][:, 0:3 * D].rearrange(
                                "p (m q) -> p m q", m=3),
                            in_=w_attn[tau, 0:3, :, :].rearrange("m p q -> p m q")),
                        (TM, tM0 + 3))
                    st["tok1"] = tok
                    st["sl1"] = (ewF.use - 1) % 2

                def wg(g):
                    with guard(g, M_PRE):
                        b_g(g)
                block.gpsimd(wg)

                gdone = {}
                gslot = {}

                def b_t(t, tau=tau):
                    ewF.wait_loaded(t, st["tok1"])
                    ph.wait(t, SM, xnn_s)
                    for m in range(3):
                        s_ = epF.acquire(t, ptoks[m])
                        for h in range(2):
                            for k in range(KD):
                                inst = t.matmul(
                                    ps[s_][:, h * TH:(h + 1) * TH],
                                    arena[st["sl1"]][:, m * D + k * 128:m * D + (k + 1) * 128],
                                    xnn_bf[:, k * T + h * TH:k * T + h * TH + TH],
                                    start=(k == 0), stop=(k == KD - 1))
                        gdone[m] = ph.inc(inst, TM)
                        gslot[m] = s_

                def wt(t):
                    with guard(t, M_PRE):
                        b_t(t)
                block.tensor(wt)

                sc = {}

                def b_s(s, tau=tau):
                    ph.wait(s, VM, mark.get("pre_scan_v", 0))
                    ph.wait(s, TM, gdone[0])
                    i = s.activation(w3[:, :], ps[gslot[0]][:, :], AF.Sigmoid,
                                     bias=ab_sb[:, tau:tau + 1])
                    sc["a"] = ph.inc(i, SM)
                    s.wait_ge(ph.sems[SM], ph.cnt[SM])
                    ph.wait(s, TM, gdone[2])
                    i = s.activation(w4[:, :], ps[gslot[2]][:, :], AF.Sigmoid)
                    sc["sd"] = ph.inc(i, SM)
                    s.wait_ge(ph.sems[SM], ph.cnt[SM])

                def ws(s):
                    with guard(s, M_PRE):
                        b_s(s)
                block.scalar(ws)

                def b_v(v, tau=tau):
                    def step(inst):
                        ph.inc(inst, VM)
                        v.wait_ge(ph.sems[VM], ph.cnt[VM])
                    ph.wait(v, SM, sc["sd"])
                    ph.wait(v, TM, gdone[1])
                    step(v.tensor_mul(w5[:, :], ps[gslot[1]][:, :], w4[:, :]))
                    assert ph.cnt[VM] == vm0 + 1
                    ph.wait(v, SM, sc["a"])
                    step(v.tensor_tensor_scan(w4[:, :], w3[:, :], w5[:, :],
                                              u0rc[:, tau:tau + 1],
                                              ALU.mult, ALU.add))
                    step(v.tensor_copy(preU[:, tau:tau + 1], w4[:, 1023:1024]))
                    mark["pre_scan_v"] = ph.cnt[VM]

                def wv(v):
                    with guard(v, M_PRE):
                        b_v(v)
                block.vector(wv)

            pre_v_fin = ph.cnt[ens.VM]

            @block.sync
            def _(y):
                with guard(y, M_D):
                    y.wait_ge(ph.sems[ens.VM], pre_v_fin)
                    y.dma_start(out=sh_pay0.ap(), in_=preU[:, :]
                                ).then_inc(ph.sems["eqS"], 16)
                    y.wait_ge(ph.sems["eqS"], 16)
                    y.dma_start(out=sh_flag0.ap(), in_=flag_sb[:, :]
                                ).then_inc(ph.sems["eqS"], 16)
                    y.wait_ge(ph.sems["eqS"], 32)

            @block.vector
            def _(v):
                with guard(v, M_C):
                    v.wait_ge(ph.sems[ens.VM], pre_v_fin)
                    i = v.tensor_copy(carry_sb[:, :], preU[:, :])
                    ph.inc(i, "eCar")

        _pn = [0]

        def poll(y, flag_dram, sem_w):
            _pn[0] += 1
            p = _pn[0]
            with y.register(f"nonce{p}") as nonce, \
                 y.register(f"fl{p}") as fl, \
                 y.register(f"it{p}") as it, \
                 y.register(f"wv{p}") as wv, \
                 y.register(f"d1{p}") as d1, \
                 y.register(f"cont{p}") as cont:
                y.reg_load(nonce, meta_sb[0:1, M_NONCE:M_NONCE + 1])
                y.reg_mov(fl, 0)
                y.reg_mov(it, 0)
                y.reg_mov(wv, ph.cnt[sem_w])
                y.reg_mov(cont, 1)
                with y.While(cont):
                    y.reg_alu(wv, wv, 16, ALU.add)
                    y.dma_start(out=fread[:, :], in_=flag_dram.ap()
                                ).then_inc(ph.sems[sem_w], 16)
                    y.wait_ge(ph.sems[sem_w], wv)
                    y.reg_load(fl, fread[0:1, 0:1])
                    y.reg_alu(it, it, 1, ALU.add)
                    y.reg_alu(d1, fl, nonce, ALU.not_equal)
                    y.reg_alu(cont, it, 300000, ALU.is_lt)
                    y.reg_alu(cont, cont, d1, ALU.bitwise_and)

        def pre_seed():
            @block.vector
            def _(v):
                i = v.memset(u0rc[:, :], 0.0)
                ph.inc(i, "vM")
                mark["u0z"] = ph.cnt["vM"]

            @block.sync
            def _(y):
                ph.wait(y, "vM", mark["u0z"])
                with guard(y, M_C):
                    poll(y, sh_flag0, "eqF")
                    y.dma_start(out=u0rc[:, :], in_=sh_pay0.ap()
                                ).then_inc(ph.sems["eqP"], 16)
                    y.wait_ge(ph.sems["eqP"], 16)
                    y.dma_start(out=fread[:, :], in_=sh_flag0.ap()
                                ).then_inc(ph.sems["eqP"], 16)
                    y.wait_ge(ph.sems["eqP"], 32)

            @block.vector
            def _(v):
                with guard(v, M_C):
                    v.wait_ge(ph.sems["eqP"], 16)

        def pre_fences():
            fins = [(ens.TM, ph.cnt[ens.TM]), (ens.VM, ph.cnt[ens.VM]),
                    (ens.SM, ph.cnt[ens.SM])]

            for engf in [block.tensor, block.vector, block.scalar,
                         block.gpsimd, block.sync]:
                def body(eng, fins=fins):
                    with guard(eng, M_PRE):
                        for sname, val in fins:
                            ph.wait(eng, sname, val)
                engf(body)

        t2w_tok = {}

        def emit_top2_A(tcx):
            vb_c = ph.cnt["vM"]

            @block.tensor
            def _(t):
                ph.wait(t, "sM", mark["lg"])
                sT = pF.acquire(t, ("vM", vb_c + 1))
                i = t.transpose(ps[sT][:, 0:8],
                                lg_sb[0:8, tcx * 128:(tcx + 1) * 128],
                                id_sb[0:8, 0:8])
                mark["tpd"] = ph.inc(i, "tM")
                mark["tps"] = sT

            @block.vector
            def _(v):
                def step(inst):
                    ph.inc(inst, "vM")
                    v.wait_ge(ph.sems["vM"], ph.cnt["vM"])
                sT = mark["tps"]
                ph.wait(v, "tM", mark["tpd"])
                i = v.tensor_copy(tp["v8"][:, :], ps[sT][:, 0:8])
                step(i)
                assert ph.cnt["vM"] == vb_c + 1
                step(v.reduce_max(tc["m1"][:, :], tp["v8"][:, :], axis=AX.X))
                step(v.tensor_scalar(tc["nm1"][:, :], tc["m1"][:, :], -1.0,
                                     None, ALU.mult))
                step(v.tensor_scalar(tp["eq"][:, :], tp["v8"][:, :],
                                     tc["m1"][:, 0:1], None, ALU.is_equal))
                step(v.reduce_sum(tc["cnt"][:, :], tp["eq"][:, :], axis=AX.X))
                step(v.scalar_tensor_tensor(tp["vm"][:, :], tp["eq"][:, :],
                                            -1e30, tp["v8"][:, :],
                                            ALU.mult, ALU.add))
                step(v.reduce_max(tc["m2"][:, :], tp["vm"][:, :], axis=AX.X))
                step(v.tensor_scalar(tc["c2"][:, :], tc["cnt"][:, :], 1.5,
                                     None, ALU.is_gt))
                step(v.tensor_copy(tc["m2p"][:, :], tc["m2"][:, :]))
                step(v.copy_predicated(tc["m2p"][:, :], tc["c2"][:, :],
                                       tc["m1"][:, :]))
                step(v.tensor_scalar(tp["mge"][:, :], tp["v8"][:, :],
                                     tc["m2p"][:, 0:1], None, ALU.is_ge))
                mark["t2m"] = ph.cnt["vM"]

            @block.scalar
            def _(s):
                def step(inst):
                    ph.inc(inst, "sM")
                    s.wait_ge(ph.sems["sM"], ph.cnt["sM"])
                ph.wait(s, "vM", mark["t2m"])
                step(s.activation(tp["ex"][:, :], tp["v8"][:, :], AF.Exp,
                                  bias=tc["nm1"][:, 0:1]))
                step(s.activation(tc["e2"][:, :], tc["m2p"][:, :], AF.Exp,
                                  bias=tc["nm1"][:, 0:1]))
                mark["t2e"] = ph.cnt["sM"]

            @block.vector
            def _(v):
                def step(inst):
                    ph.inc(inst, "vM")
                    v.wait_ge(ph.sems["vM"], ph.cnt["vM"])
                ph.wait(v, "sM", mark["t2e"])
                step(v.tensor_scalar(tc["den"][:, :], tc["e2"][:, :], 1.0,
                                     None, ALU.add))
                step(v.reciprocal(tc["rden"][:, :], tc["den"][:, :]))
                step(v.tensor_mul(t2wet[tcx][:, :], tp["ex"][:, :],
                                  tp["mge"][:, :]))
                step(v.tensor_scalar(t2wet[tcx][:, :], t2wet[tcx][:, :],
                                     tc["rden"][:, 0:1], None, ALU.mult))
                t2w_tok[tcx] = ph.cnt["vM"]

        def emit_top2_B(tcx):
            vb_c = ph.cnt["vM"]

            @block.tensor
            def _(t):
                ph.wait(t, "vM", t2w_tok[tcx])
                sT = pF.acquire(t, ("vM", vb_c + 1))
                i = t.transpose(ps[sT][0:8, 0:128], t2wet[tcx][:, :],
                                id_sb[:, :])
                mark["tpbd"] = ph.inc(i, "tM")
                mark["tpbs"] = sT

            @block.vector
            def _(v):
                sT = mark["tpbs"]
                ph.wait(v, "tM", mark["tpbd"])
                i = v.tensor_copy(we_sb[0:8, tcx * 128:(tcx + 1) * 128],
                                  ps[sT][0:8, 0:128])
                ph.inc(i, "vM")
                v.wait_ge(ph.sems["vM"], ph.cnt["vM"])
                assert ph.cnt["vM"] == vb_c + 1
            if tcx == 7:
                mark["we_done"] = ph.cnt["vM"]

        def attn_own():
            @block.scalar
            def _(s):
                pass

            xnn_v, xnn_s = mark["xnn_done"]
            for tau in range(NT):
                if 2 <= tau <= 9:
                    emit_top2_B(tau - 2)
                if 1 <= tau <= 8:
                    emit_top2_A(tau - 1)
                st = {}
                sm0 = ph.cnt["sM"]
                vm0 = ph.cnt["vM"]
                tM0 = ph.cnt["tM"]
                ptoks = [("sM", sm0 + 1), ("vM", vm0 + 1), ("sM", sm0 + 2),
                         ("vM", vm0 + 4), ("vM", vm0 + 5)]

                @block.gpsimd
                def _(g):
                    _, tok, _ = wF.load(
                        g, lambda j, tau=tau: g.dma_start(
                            out=arena[j][:, 0:3 * D].rearrange(
                                "p (m q) -> p m q", m=3),
                            in_=w_attn[tau, 0:3, :, :].rearrange("m p q -> p m q")),
                        ("tM", tM0 + 3))
                    st["tok1"] = tok
                    st["sl1"] = (wF.use - 1) % 2
                    _, tok2, _ = wF.load(
                        g, lambda j, tau=tau: g.dma_start(
                            out=arena[j][:, 0:2 * D].rearrange(
                                "p (m q) -> p m q", m=2),
                            in_=w_attn[tau, 3:5, :, :].rearrange("m p q -> p m q")),
                        ("tM", tM0 + 5))
                    st["tok2"] = tok2
                    st["sl2"] = (wF.use - 1) % 2

                gdone = {}
                gslot = {}

                @block.tensor
                def _(t):
                    wF.wait_loaded(t, st["tok1"])
                    ph.wait(t, "sM", xnn_s)
                    for m in range(5):
                        if m == 3:
                            wF.wait_loaded(t, st["tok2"])
                        sl = st["sl1"] if m < 3 else st["sl2"]
                        off = (m if m < 3 else m - 3) * D
                        s_ = pF.acquire(t, ptoks[m])
                        for h in range(2):
                            for k in range(KD):
                                inst = t.matmul(
                                    ps[s_][:, h * TH:(h + 1) * TH],
                                    arena[sl][:, off + k * 128:off + (k + 1) * 128],
                                    xnn_bf[:, k * T + h * TH:k * T + h * TH + TH],
                                    start=(k == 0), stop=(k == KD - 1))
                        gdone[m] = ph.inc(inst, "tM")
                        assert gdone[m] == tM0 + m + 1
                        gslot[m] = s_

                sc = {}

                @block.scalar
                def _(s):
                    ph.wait(s, "vM", max(mark["lg_pre_own"],
                                         mark.get("scan_v", 0)))
                    ph.wait(s, "tM", gdone[0])
                    i = s.activation(w3[:, :], ps[gslot[0]][:, :], AF.Sigmoid,
                                     bias=ab_sb[:, tau:tau + 1])
                    sc["a"] = ph.inc(i, "sM")
                    assert sc["a"] == sm0 + 1
                    s.wait_ge(ph.sems["sM"], ph.cnt["sM"])
                    ph.wait(s, "tM", gdone[2])
                    i = s.activation(w4[:, :], ps[gslot[2]][:, :], AF.Sigmoid)
                    sc["sd"] = ph.inc(i, "sM")
                    s.wait_ge(ph.sems["sM"], ph.cnt["sM"])

                @block.vector
                def _(v):
                    def step(inst):
                        ph.inc(inst, "vM")
                        v.wait_ge(ph.sems["vM"], ph.cnt["vM"])
                    ph.wait(v, "sM", sc["sd"])
                    ph.wait(v, "tM", gdone[1])
                    step(v.tensor_mul(w5[:, :], ps[gslot[1]][:, :], w4[:, :]))
                    assert ph.cnt["vM"] == vm0 + 1
                    ph.wait(v, "sM", sc["a"])
                    step(v.tensor_tensor_scan(w4[:, :], w3[:, :], w5[:, :],
                                              0.0, ALU.mult, ALU.add))
                    step(v.tensor_tensor_scan(R_[:, :], w3[:, :], w5[:, :],
                                              1.0, ALU.mult, ALU.add))
                    sc["h_done"] = ph.cnt["vM"]

                @block.scalar
                def _(s):
                    ph.wait(s, "tM", gdone[3])
                    ph.wait(s, "vM", sc["h_done"])
                    i = s.activation(w3[:, :], ps[gslot[3]][:, :], AF.Sigmoid)
                    sc["scg"] = ph.inc(i, "sM")
                    s.wait_ge(ph.sems["sM"], ph.cnt["sM"])

                @block.vector
                def _(v):
                    def step(inst):
                        ph.inc(inst, "vM")
                        v.wait_ge(ph.sems["vM"], ph.cnt["vM"])
                    ph.wait(v, "sM", sc["scg"])
                    step(v.tensor_mul(w5[:, :], ps[gslot[3]][:, :], w3[:, :]))
                    assert ph.cnt["vM"] == vm0 + 4
                    ph.wait(v, "tM", gdone[4])
                    step(v.tensor_mul(w5[:, :], w5[:, :], ps[gslot[4]][:, :]))
                    assert ph.cnt["vM"] == vm0 + 5
                    ph.wait(v, "tM", mark.get("ffn_mm_done", 0))
                    step(v.tensor_mul(ye_bf[:, tau * T:(tau + 1) * T],
                                      w5[:, :], w4[:, :]))
                    step(v.tensor_sub(R_[:, :], R_[:, :], w4[:, :]))
                    step(v.tensor_mul(acc_bf[:, tau * T:(tau + 1) * T],
                                      w5[:, :], R_[:, :]))
                    step(v.tensor_copy(preU[:, tau:tau + 1], w4[:, 1023:1024]))
                    step(v.tensor_copy(preA[:, tau:tau + 1], R_[:, 1023:1024]))
                    mark["scan_v"] = ph.cnt["vM"]

        def p4_phase():
            @block.sync
            def _(y):
                with guard(y, M_B):
                    poll(y, sh_flag0, "eqF")
                    y.dma_start(out=carry_sb[:, :], in_=sh_pay0.ap()
                                ).then_inc(ph.sems["eCar"], 16)
                    y.wait_ge(ph.sems["eCar"], 16)
                with guard(y, M_D):
                    poll(y, sh_flag1, "eqF")
                    y.dma_start(out=carry_sb[:, :], in_=sh_pay1.ap()
                                ).then_inc(ph.sems["eCar"], 16)
                    y.wait_ge(ph.sems["eCar"], 16)

            p3_t_fin = ph.cnt["tM"]

            @block.vector
            def _(v):
                def step(inst):
                    ph.inc(inst, "vM")
                    v.wait_ge(ph.sems["vM"], ph.cnt["vM"])
                with guard(v, M_CW):
                    v.wait_ge(ph.sems["eCar"], 1)
                ph.wait(v, "tM", p3_t_fin)
                for tau in range(NT):
                    step(v.scalar_tensor_tensor(
                        xnn_bf[:, tau * T:(tau + 1) * T],
                        acc_bf[:, tau * T:(tau + 1) * T],
                        carry_sb[:, tau:tau + 1],
                        ye_bf[:, tau * T:(tau + 1) * T],
                        ALU.mult, ALU.add))
                step(v.tensor_mul(w3[:, 0:NT], preA[:, :], carry_sb[:, :]))
                step(v.tensor_add(state[:, :], preU[:, :], w3[:, 0:NT]))
                mark["scan_v"] = ph.cnt["vM"]
                mark["y_done"] = ph.cnt["vM"]

            @block.sync
            def _(y):
                with guard(y, M_A):
                    y.wait_ge(ph.sems["vM"], mark["y_done"])
                    y.dma_start(out=sh_pay0.ap(), in_=state[:, :]
                                ).then_inc(ph.sems["eqS"], 16)
                    y.wait_ge(ph.sems["eqS"], 16)
                    y.dma_start(out=sh_flag0.ap(), in_=flag_sb[:, :]
                                ).then_inc(ph.sems["eqS"], 16)
                    y.wait_ge(ph.sems["eqS"], 32)
                with guard(y, M_C):
                    y.wait_ge(ph.sems["vM"], mark["y_done"])
                    y.dma_start(out=sh_pay1.ap(), in_=state[:, :]
                                ).then_inc(ph.sems["eqS"], 16)
                    y.wait_ge(ph.sems["eqS"], 16)
                    y.dma_start(out=sh_flag1.ap(), in_=flag_sb[:, :]
                                ).then_inc(ph.sems["eqS"], 16)
                    y.wait_ge(ph.sems["eqS"], 32)

        def ffn_phase():
            @block.scalar
            def _(s):
                pass

            for e in range(E):
                st = {}

                @block.sync
                def _(y):
                    _, tok, uidx = fbF.load(
                        y, lambda j, e=e: y.dma_start(out=fb_sb[j][:, :],
                                                      in_=fbias[e, :, :]),
                        None)
                    st["tok"] = tok
                    st["uidx"] = uidx
                    st["fb"] = uidx % 2

                @block.vector
                def _(v):
                    sl = st["fb"]
                    fbF.wait_loaded(v, st["tok"])
                    i = v.tensor_scalar(fb2_sb[sl][:, 0:16], fb_sb[sl][:, 0:16],
                                        1.702, None, ALU.mult)
                    ph.inc(i, "vM")
                    v.wait_ge(ph.sems["vM"], ph.cnt["vM"])
                    i = v.tensor_scalar(fb2_sb[sl][:, 16:32], fb_sb[sl][:, 16:32],
                                        1.0, None, ALU.add)
                    ph.inc(i, "vM")
                    v.wait_ge(ph.sems["vM"], ph.cnt["vM"])
                    mark["fbr"] = ph.cnt["vM"]

                vbw = ph.cnt["vM"]

                @block.tensor
                def _(t):
                    ph.wait(t, "vM", mark["we_done"])
                    sW = pF.acquire(t, ("vM", vbw + 1))
                    for h in range(2):
                        i = t.matmul(ps[sW][:, h * TH:(h + 1) * TH],
                                     oh_sb[0:8, e * 128:(e + 1) * 128],
                                     we_sb[0:8, h * TH:(h + 1) * TH],
                                     start=True, stop=True)
                    mark["w8d"] = ph.inc(i, "tM")
                    mark["w8s"] = sW

                @block.vector
                def _(v):
                    sW = mark["w8s"]
                    ph.wait(v, "tM", mark["w8d"])
                    i = v.tensor_copy(w8_sb[:, :], ps[sW][:, :])
                    v_ = ph.inc(i, "vM")
                    assert v_ == vbw + 1
                    v.wait_ge(ph.sems["vM"], ph.cnt["vM"])
                    mark["w8r"] = ph.cnt["vM"]

                for ti in range(KI):
                    stt = {}
                    tb = ph.cnt["tM"]
                    sb = ph.cnt["sM"]

                    @block.gpsimd
                    def _(g):
                        _, tok, _ = wF.load(
                            g,
                            lambda j, e=e, ti=ti: g.dma_start(
                                out=arena[j][:, 0:2 * D].rearrange(
                                    "p (c q) -> p c q", c=2),
                                in_=w1p[e, ti, :, :, :].rearrange("c p q -> p c q")),
                            ("tM", tb + 2))
                        stt["tok"] = tok
                        stt["sl"] = (wF.use - 1) % 2

                    gdone = {}
                    gslot = {}

                    @block.tensor
                    def _(t):
                        sl = stt["sl"]
                        wF.wait_loaded(t, stt["tok"])
                        ptk = [("sM", sb + 2), ("sM", sb + 3)]
                        for m in range(2):
                            s_ = pF.acquire(t, ptk[m])
                            for h in range(2):
                                for k in range(KD):
                                    inst = t.matmul(
                                        ps[s_][:, h * TH:(h + 1) * TH],
                                        arena[sl][:, m * D + k * 128:m * D + (k + 1) * 128],
                                        xn_bf[:, k * T + h * TH:k * T + h * TH + TH],
                                        start=(k == 0), stop=(k == KD - 1))
                            gdone[m] = ph.inc(inst, "tM")
                            assert gdone[m] == tb + m + 1
                            gslot[m] = s_

                    sc = {}

                    @block.scalar
                    def _(s):
                        sl = st["fb"]
                        ph.wait(s, "vM", max(mark["fbr"], mark.get("fpool", 0)))
                        ph.wait(s, "tM", gdone[0])
                        i = s.activation(w3[:, :], ps[gslot[0]][:, :], AF.Sigmoid,
                                         bias=fb2_sb[sl][:, ti:ti + 1], scale=1.702)
                        sc["st"] = ph.inc(i, "sM")
                        assert sc["st"] == sb + 1
                        s.wait_ge(ph.sems["sM"], ph.cnt["sM"])
                        i = s.activation(w4[:, :], ps[gslot[0]][:, :], AF.Identity,
                                         bias=fb_sb[sl][:, ti:ti + 1])
                        sc["xg"] = ph.inc(i, "sM")
                        s.wait_ge(ph.sems["sM"], ph.cnt["sM"])
                        ph.wait(s, "tM", gdone[1])
                        i = s.activation(w5[:, :], ps[gslot[1]][:, :], AF.Identity,
                                         bias=fb2_sb[sl][:, 16 + ti:16 + ti + 1])
                        sc["xl"] = ph.inc(i, "sM")
                        s.wait_ge(ph.sems["sM"], ph.cnt["sM"])

                    @block.vector
                    def _(v):
                        ph.wait(v, "sM", sc["xg"])
                        i = v.tensor_mul(w3[:, :], w4[:, :], w3[:, :])
                        ph.inc(i, "vM")
                        v.wait_ge(ph.sems["vM"], ph.cnt["vM"])
                        ph.wait(v, "sM", sc["xl"])
                        i = v.tensor_mul(ye_bf[:, ti * T:(ti + 1) * T],
                                         w3[:, :], w5[:, :])
                        ph.inc(i, "vM")
                        v.wait_ge(ph.sems["vM"], ph.cnt["vM"])
                        mark["fpool"] = ph.cnt["vM"]
                        mark[f"ye{ti}"] = ph.cnt["vM"]

                for th in range(KH):
                    stt = {}
                    tb = ph.cnt["tM"]
                    vbh = ph.cnt["vM"]

                    @block.gpsimd
                    def _(g):
                        _, tok, _ = wF.load(
                            g,
                            lambda j, e=e, th=th: g.dma_start(
                                out=arena[j][:, 0:D], in_=w2p[e, th, :, :]),
                            ("tM", tb + 1))
                        stt["tok"] = tok
                        stt["sl"] = (wF.use - 1) % 2

                    @block.tensor
                    def _(t):
                        sl = stt["sl"]
                        wF.wait_loaded(t, stt["tok"])
                        ph.wait(t, "vM", mark[f"ye{KI - 1}"])
                        s_ = pF.acquire(t, ("vM", vbh + 1))
                        for h in range(2):
                            for k in range(KI):
                                inst = t.matmul(
                                    ps[s_][:, h * TH:(h + 1) * TH],
                                    arena[sl][:, k * 128:(k + 1) * 128],
                                    ye_bf[:, k * T + h * TH:k * T + h * TH + TH],
                                    start=(k == 0), stop=(k == KI - 1))
                        v_ = ph.inc(inst, "tM")
                        assert v_ == tb + 1
                        mark["w2d"] = v_
                        mark["w2s"] = s_

                    @block.vector
                    def _(v):
                        s_ = mark["w2s"]
                        sl = st["fb"]
                        ph.wait(v, "tM", mark["w2d"])
                        ph.wait(v, "vM", mark["w8r"])
                        i = v.scalar_tensor_tensor(w4[:, :], ps[s_][:, :],
                                                   fb_sb[sl][:, 32 + th:32 + th + 1],
                                                   w8_sb[:, :], ALU.add, ALU.mult)
                        v_ = ph.inc(i, "vM")
                        assert v_ == vbh + 1
                        v.wait_ge(ph.sems["vM"], ph.cnt["vM"])
                        if th == KH - 1:
                            fbF.set_rel(st["uidx"], ("vM", v_))
                        dst = acc_bf[:, th * T:(th + 1) * T]
                        if e == 0:
                            i = v.tensor_copy(dst, w4[:, :])
                        else:
                            i = v.tensor_add(dst, dst, w4[:, :])
                        ph.inc(i, "vM")
                        v.wait_ge(ph.sems["vM"], ph.cnt["vM"])
            mark["ffn_mm_done"] = ph.cnt["tM"]
            mark["ffn_v_done"] = ph.cnt["vM"]

        def out_phase():
            @block.scalar
            def _(s):
                pass

            for td in range(KD):
                stt = {}
                tb = ph.cnt["tM"]
                sb = ph.cnt["sM"]
                buf = xf if td % 2 == 0 else xg

                @block.gpsimd
                def _(g):
                    if td % 2 == 0:
                        _, tok, _ = wF.load(
                            g,
                            lambda j, td=td: g.dma_start(
                                out=arena[j][:, 0:2 * D].rearrange(
                                    "p (a q) -> p a q", a=2),
                                in_=w_out[td:td + 2, :, :].rearrange(
                                    "a p q -> p a q")),
                            ("tM", tb + 2))
                        mark["otok"] = tok
                        mark["ows"] = (wF.use - 1) % 2

                @block.sync
                def _(y):
                    if td == 0:
                        ph.wait_tok(y, mark.get("xf_rel"))
                    if td == 1:
                        ph.wait_tok(y, mark.get("xg_rel"))
                    if td >= 2:
                        ph.wait(y, "qO", 16 * (td - 1))
                    qn = "qXF" if td % 2 == 0 else "qXG"
                    y.dma_start(out=buf[:, :],
                                in_=x_t[td * 128:(td + 1) * 128, NPRE * T:(NPRE + 1) * T]
                                ).then_inc(ph.sems[qn], 16)
                    ph.cnt[qn] += 16
                    stt["qx"] = (qn, ph.cnt[qn])

                @block.tensor
                def _(t):
                    sl = mark["ows"]
                    wF.wait_loaded(t, mark["otok"])
                    ph.wait(t, "vM", mark["scan_v"])
                    off = (td % 2) * D
                    s_ = pF.acquire(t, ("sM", sb + 1))
                    for h in range(2):
                        for k in range(NT):
                            inst = t.matmul(
                                ps[s_][:, h * TH:(h + 1) * TH],
                                arena[sl][:, off + k * 128:off + (k + 1) * 128],
                                xnn_bf[:, k * T + h * TH:k * T + h * TH + TH],
                                start=(k == 0), stop=(k == NT - 1))
                    mark["od"] = ph.inc(inst, "tM")
                    mark["os"] = s_

                sc = {}
                s1b = w3 if td % 2 == 0 else R_
                gb = w4 if td % 2 == 0 else w8_sb

                @block.scalar
                def _(s):
                    s_ = mark["os"]
                    ph.wait(s, "vM", max(mark.get(f"comb{td - 2}", 0),
                                         mark["scan_v"]))
                    ph.wait(s, "tM", mark["od"])
                    i = s.activation(s1b[:, :], ps[s_][:, :], AF.Identity,
                                     bias=ob_sb[:, td:td + 1])
                    sc["s1"] = ph.inc(i, "sM")
                    assert sc["s1"] == sb + 1
                    s.wait_ge(ph.sems["sM"], ph.cnt["sM"])
                    ph.wait(s, "vM", mark["ffn_v_done"])
                    i = s.activation(gb[:, :], acc_bf[:, td * T:(td + 1) * T],
                                     AF.Silu)
                    sc["g"] = ph.inc(i, "sM")
                    s.wait_ge(ph.sems["sM"], ph.cnt["sM"])

                @block.gpsimd
                def _(g):
                    ph.wait(g, "sM", sc["s1"])
                    ph.wait(g, "vM", mark.get(f"comb{td - 2}", 0))
                    i = g.tensor_sub(s1b[:, :], s1b[:, :], Q_[:, :])
                    sc["gq"] = ph.inc(i, "gM")
                    g.wait_ge(ph.sems["gM"], ph.cnt["gM"])

                @block.vector
                def _(v):
                    def step(inst):
                        ph.inc(inst, "vM")
                        v.wait_ge(ph.sems["vM"], ph.cnt["vM"])
                    ph.wait(v, stt["qx"][0], stt["qx"][1])
                    step(v.tensor_mul(w5[:, :], buf[:, :], P_[:, :]))
                    ph.wait(v, "gM", sc["gq"])
                    step(v.tensor_add(w5[:, :], w5[:, :], s1b[:, :]))
                    ph.wait(v, "sM", sc["g"])
                    step(v.tensor_mul(w5[:, :], w5[:, :], gb[:, :]))
                    step(v.tensor_add(buf[:, :], buf[:, :], w5[:, :]))
                    mark["comb"] = ph.cnt["vM"]
                    mark[f"comb{td}"] = ph.cnt["vM"]

                @block.sync
                def _(y):
                    ph.wait(y, "vM", mark["comb"])
                    y.dma_start(out=o[td * 128:(td + 1) * 128, :], in_=buf[:, :]
                                ).then_inc(ph.sems["qO"], 16)
                    ph.cnt["qO"] += 16

            @block.sync
            def _(y):
                y.wait_ge(ph.sems["qO"], ph.cnt["qO"])

        def finale_stub():
            @block.vector
            def _(v):
                i = v.memset(w3[:, :], 0.5)
                ph.inc(i, "vM")
                mark["stub"] = ph.cnt["vM"]

            @block.sync
            def _(y):
                ph.wait(y, "vM", mark["stub"])
                for td in range(KD):
                    y.dma_start(out=o[td * 128:(td + 1) * 128, :], in_=w3[:, :]
                                ).then_inc(ph.sems["qO"], 16)
                    ph.cnt["qO"] += 16
                y.wait_ge(ph.sems["qO"], ph.cnt["qO"])

        def norm_own():
            mark["xnn_done"] = norm_generic(T, True, NS(ph, ""), xbF, pF, None)

        def d_fence():
            @block.vector
            def _(v):
                with guard(v, M_D):
                    v.wait_ge(ph.sems["eqS"], 32)

        steps = [
            ("pre_seed", pre_seed),
            ("pre", pre_phase),
            ("fences", pre_fences),
            ("norm_own", norm_own),
            ("d_fence", d_fence),
            ("attn_own", attn_own),
            ("p4", p4_phase),
            ("ffn", ffn_phase),
            ("out", out_phase),
        ]
        done_all = True
        for name, fn in steps:
            fn()
            if stop_after is not None and name == stop_after:
                done_all = False
                break
        if not done_all:
            finale_stub()

    return nc


def kernel(L0_attn_x, L0_attn_rms_w, L0_attn_A_w, L0_attn_A_b, L0_attn_B_w,
           L0_attn_Ca_w, L0_attn_Cb_w, L0_attn_D_w, L0_attn_out_w, L0_attn_out_b,
           L0_ffn_gate_w, L0_ffn_gate_b, L0_ffn_w1, L0_ffn_b1, L0_ffn_w2,
           L0_ffn_b2):
    import secrets
    B, L, _ = L0_attn_x.shape
    x2 = np.ascontiguousarray(np.asarray(L0_attn_x, np.float32).reshape(B * L, D))

    nonce = (secrets.randbits(31) | 1)
    xslabs = []
    metas = []
    for c in range(NCORES):
        beta, r = c // 4, c % 4
        own0 = beta * L + r * T
        slab = np.zeros((2 * T, D), np.float32)
        if r == 2:
            slab[0:T] = x2[beta * L + T:beta * L + 2 * T]
        elif r == 3:
            slab[0:T] = x2[beta * L:beta * L + T]
        slab[T:] = x2[own0:own0 + T]
        xslabs.append(np.ascontiguousarray(slab.T))
        has_pre = 1 if r >= 2 else 0
        cw = 1 if r in (1, 2, 3) else 0
        metas.append(np.array([[r, nonce, has_pre,
                                1 if r == 0 else 0, 1 if r == 1 else 0,
                                1 if r == 2 else 0, 1 if r == 3 else 0,
                                cw]], np.int32))

    def tileT(w):
        n, d = w.shape
        return np.ascontiguousarray(
            w.reshape(n // 128, 128, d // 128, 128).transpose(0, 3, 2, 1))

    wa = np.stack([tileT(np.asarray(w, np.float32)) for w in
                   [L0_attn_A_w, L0_attn_B_w, L0_attn_D_w, L0_attn_Ca_w,
                    L0_attn_Cb_w]], axis=1)
    wa = np.ascontiguousarray(wa.reshape(NT, 5, 128, D))
    wo = np.ascontiguousarray(
        tileT(np.asarray(L0_attn_out_w, np.float32)).reshape(KD, 128, D))
    wg = np.ascontiguousarray(
        np.asarray(L0_ffn_gate_w, np.float32).reshape(E, KD, 128)
        .transpose(2, 1, 0).reshape(128, KD * E))
    w1 = np.asarray(L0_ffn_w1, np.float32)
    w1g = np.ascontiguousarray(w1[:, 0::2, :])
    w1l = np.ascontiguousarray(w1[:, 1::2, :])

    def tileT_e(w):
        eN, iN, dN = w.shape
        return w.reshape(eN, iN // 128, 128, dN // 128, 128).transpose(0, 1, 4, 3, 2)

    w1pk = np.ascontiguousarray(
        np.stack([tileT_e(w1g), tileT_e(w1l)], axis=2).reshape(E, KI, 2, 128, D))
    w2 = np.asarray(L0_ffn_w2, np.float32)
    w2pk = np.ascontiguousarray(
        w2.reshape(E, KH, 128, KI, 128).transpose(0, 1, 4, 3, 2).reshape(E, KH, 128, D))

    col = lambda v: np.ascontiguousarray(np.asarray(v, np.float32).reshape(-1, 128).T)
    ab = col(L0_attn_A_b)
    ob = col(L0_attn_out_b)
    rr = col(L0_attn_rms_w)
    gb = np.ascontiguousarray(np.asarray(L0_ffn_gate_b, np.float32).reshape(8, 1))
    b1 = np.asarray(L0_ffn_b1, np.float32)
    b2 = np.asarray(L0_ffn_b2, np.float32)
    fb = np.stack([col(b1[e, 0::2]) for e in range(E)])
    fbl = np.stack([col(b1[e, 1::2]) for e in range(E)])
    fb2 = np.stack([col(b2[e]) for e in range(E)])
    fbias = np.ascontiguousarray(np.concatenate([fb, fbl, fb2], axis=2))
    idm = np.eye(128, dtype=np.float32)
    oh = np.zeros((8, E * 128), np.float32)
    for e in range(E):
        oh[e, e * 128:(e + 1) * 128] = 1.0

    nc = build_program(stop_after=_TEST_STOP)
    shared = dict(w_attn=wa, w_out=wo, w_gate=wg, w1p=w1pk, w2p=w2pk,
                  a_bias=ab, o_bias=ob, rms_r=rr, g_bias=gb, fbias=fbias,
                  ident=idm, onehot=oh)
    in_maps = [dict(shared, x_t=xslabs[c], meta=metas[c]) for c in range(NCORES)]
    res = run_bass_kernel_spmd(nc, in_maps, core_ids=list(range(NCORES)))
    out = np.concatenate([res.results[c]["o"] for c in range(NCORES)], axis=1)
    return np.ascontiguousarray(out.T).reshape(B, L, D)
